# version 04
# baseline (speedup 1.0000x reference)
import contextlib
import numpy as np
import concourse.bass as bass
import concourse.mybir as mybir
from concourse.bass_utils import run_bass_kernel_spmd

F32 = mybir.dt.float32
BF16 = mybir.dt.bfloat16
AF = mybir.ActivationFunctionType
ALU = mybir.AluOpType
AX = mybir.AxisListType

B, S, D, H, DH, KC = 4, 4096, 1024, 4, 256, 4
UP = 1344
FFI = 2816
EPS = 1e-5
NCORES = 8
NT = 2048
PC = D // 128
MID = [0.25, 0.75, 1.25, 1.75, 2.5, 3.5, 5.0]
GSTEP = [0.5, 0.5, 0.5, 0.5, 1.0, 1.0, 2.0]



class Script:

    def __init__(self):
        self.steps = []

    def add(self, eng, fn, w=1, dma=False):
        self.steps.append((eng, fn, w, dma))

    def t(self, fn):
        self.add("tensor", fn, 1)

    def v(self, fn):
        self.add("vector", fn, 1)

    def s(self, fn):
        self.add("scalar", fn, 1)

    def g(self, fn):
        self.add("gpsimd", fn, 1)

    def y(self, fn, w=16):
        self.add("sync", fn, w, dma=True)


def emit_script(nc, es, sc):
    sem_c = es.enter_context(nc.semaphore("SCRIPT_C"))
    sem_d = es.enter_context(nc.semaphore("SCRIPT_D"))
    n = len(sc.steps)
    cumc = [0] * (n + 1)
    cumd = [0] * (n + 1)
    for i, (_, _, w, dma) in enumerate(sc.steps):
        cumc[i + 1] = cumc[i] + (0 if dma else w)
        cumd[i + 1] = cumd[i] + (w if dma else 0)
    engs = []
    for engname in ("sync", "tensor", "vector", "scalar", "gpsimd"):
        if any(e == engname for e, _, _, _ in sc.steps):
            engs.append(engname)
    with nc.Block() as block:
        for engname in engs:
            def make(engname=engname):
                def closure(eng):
                    prev = None
                    for i, (en, fn, w, dma) in enumerate(sc.steps):
                        if en != engname:
                            continue
                        if cumc[i] > 0 and prev != cumc[i]:
                            eng.wait_ge(sem_c, cumc[i])
                        if cumd[i] > 0:
                            eng.wait_ge(sem_d, cumd[i])
                        added = [0]

                        def emit(inst, k=None, added=added, dma=dma):
                            k = (16 if dma else 1) if k is None else k
                            if dma:
                                inst.then_inc(sem_d, k)
                            added[0] += k
                        fn(eng, emit)
                        assert added[0] == w, (
                            "step %d weight %d != emitted %d" % (i, w, added[0]))
                        if not dma:
                            eng.maybe_drain_then_inc((sem_c, w), fusable=True)
                        prev = cumc[i + 1]
                    eng.wait_ge(sem_c, cumc[n])
                    eng.wait_ge(sem_d, cumd[n])
                return closure
            getattr(block, engname)(make())


class Alloc:
    def __init__(self, nc, es):
        self.nc = nc
        self.es = es

    def sb(self, name, shape, dtype=F32):
        return self.es.enter_context(self.nc.sbuf_tensor(name, shape, dtype))

    def ps(self, name, shape, dtype=F32):
        return self.es.enter_context(self.nc.psum_tensor(name, shape, dtype))


def _recip(eng, out, in_):
    eng.drain()
    eng.reciprocal(out, in_)
    eng.drain()


def _chunks(total, step=512):
    out, c = [], 0
    while c < total:
        out.append((c, min(step, total - c)))
        c += step
    return out



def make_ln_shared(al, maxcols):
    del maxcols
    sh = {}
    sh["sq"] = al.sb("lnsq", [128, 512])
    sh["st"] = al.sb("lnst", [1, 2 * 512])
    sh["tm"] = al.sb("lntm", [1, 2 * 512])
    sh["ab"] = al.sb("lnab", [1, 2 * 512])
    sh["psS"] = al.ps("lnpsS", [1, 512])
    sh["psQ"] = al.ps("lnpsQ", [1, 512])
    sh["psA"] = al.ps("lnpsA", [128, 512])
    sh["psB"] = al.ps("lnpsB", [128, 512])
    return sh


def add_ln(sc, sh, xt, ot, ncols, PCn, ones, onerow, w_sb=None, nfeat=None):
    nfeat = nfeat or (PCn * 128)
    sq, st, tm, ab = sh["sq"], sh["st"], sh["tm"], sh["ab"]
    psS, psQ, psA, psB = sh["psS"], sh["psQ"], sh["psA"], sh["psB"]

    for c0, w in _chunks(ncols):
        for pc in range(PCn):
            def s_sq(eng, emit, pc=pc, c0=c0, w=w):
                emit(eng.activation(sq[:, 0:w],
                                    xt[:, pc * ncols + c0:pc * ncols + c0 + w],
                                    AF.Square))
            sc.s(s_sq)

            def t_mm(eng, emit, pc=pc, c0=c0, w=w):
                eng.matmul(psS[:, 0:w], ones[:, :],
                           xt[:, pc * ncols + c0:pc * ncols + c0 + w],
                           start=(pc == 0), stop=(pc == PCn - 1))
                emit(eng.matmul(psQ[:, 0:w], ones[:, :], sq[:, 0:w],
                                start=(pc == 0), stop=(pc == PCn - 1)))
            sc.t(t_mm)

        def v_stats(eng, emit, w=w):
            mu = tm[:, 0:w]
            vv = tm[:, 512:512 + w]
            eng.tensor_scalar_mul(mu, psS[:, 0:w], 1.0 / nfeat)
            eng.tensor_scalar_mul(vv, psQ[:, 0:w], 1.0 / nfeat)
            eng.tensor_mul(st[:, 0:w], mu, mu)
            eng.tensor_sub(vv, vv, st[:, 0:w])
            emit(eng.tensor_scalar_add(vv, vv, EPS))
        sc.v(v_stats)

        def s_sqrt(eng, emit, w=w):
            emit(eng.activation(ab[:, 0:w], tm[:, 512:512 + w], AF.Sqrt))
        sc.s(s_sqrt)

        def v_newton(eng, emit, w=w):
            vv = tm[:, 512:512 + w]
            s0 = ab[:, 0:w]
            t0 = st[:, 512:512 + w]
            for _ in range(2):
                _recip(eng, t0, s0)
                eng.tensor_mul(t0, vv, t0)
                eng.tensor_add(t0, t0, s0)
                eng.tensor_scalar_mul(s0, t0, 0.5)
            _recip(eng, ab[:, 0:w], s0)
            eng.tensor_mul(ab[:, 512:512 + w], tm[:, 0:w], ab[:, 0:w])
            emit(eng.tensor_scalar_mul(ab[:, 512:512 + w],
                                       ab[:, 512:512 + w], -1.0))
        sc.v(v_newton)

        def t_bc(eng, emit, c0=c0, w=w):
            eng.matmul(psA[:, 0:w], onerow[:, :], ab[:, 0:w],
                       start=True, stop=True)
            emit(eng.matmul(psB[:, 0:w], onerow[:, :], ab[:, 512:512 + w],
                            start=True, stop=True))
        sc.t(t_bc)

        def v_app(eng, emit, c0=c0, w=w):
            for pc in range(PCn):
                sl = slice(pc * ncols + c0, pc * ncols + c0 + w)
                eng.tensor_mul(ot[:, sl], xt[:, sl], psA[:, 0:w])
                i = eng.tensor_add(ot[:, sl], ot[:, sl], psB[:, 0:w])
                if w_sb is not None:
                    i = eng.tensor_scalar_mul(ot[:, sl], ot[:, sl],
                                              w_sb[:, pc:pc + 1])
            emit(i)
        sc.v(v_app)


def add_consts(sc, al):
    ones = al.sb("ones_c", [128, 1])
    onerow = al.sb("onerow_c", [1, 128])

    def v_init(eng, emit):
        eng.memset(ones[:, :], 1.0)
        emit(eng.memset(onerow[:, :], 1.0))
    sc.v(v_init)
    return ones, onerow


def add_load_fm(sc, dst, src, ncols, PCn):
    def y_ld(eng, emit):
        for pc in range(PCn):
            emit(eng.dma_start(out=dst[:, pc * ncols:(pc + 1) * ncols],
                               in_=src[pc * 128:(pc + 1) * 128, :]), 16)
    sc.y(y_ld, 16 * PCn)


def add_store_fm(sc, dst, src_sb, ncols, PCn):
    def g_st(eng, emit):
        for pc in range(PCn):
            emit(eng.dma_start(out=dst[pc * 128:(pc + 1) * 128, :],
                               in_=src_sb[:, pc * ncols:(pc + 1) * ncols]), 16)
    sc.y(g_st, 16 * PCn)



def build_prep():
    nc = bass.Bass("TRN2", target_bir_lowering=False,
                   detect_race_conditions=False)
    x = nc.dram_tensor("x", [D, NT], F32, kind="ExternalInput")
    xn = nc.dram_tensor("xn", [D, NT], F32, kind="ExternalOutput")
    with contextlib.ExitStack() as es:
        al = Alloc(nc, es)
        sc = Script()
        xt = al.sb("xt", [128, PC * NT])
        ones, onerow = add_consts(sc, al)
        add_load_fm(sc, xt, x, NT, PC)
        sh = make_ln_shared(al, NT)
        add_ln(sc, sh, xt, xt, NT, PC, ones, onerow)
        add_store_fm(sc, xn, xt, NT, PC)
        emit_script(nc, es, sc)
    return nc



def build_tmq(ntiles, C):
    nc = bass.Bass("TRN2", target_bir_lowering=False,
                   detect_race_conditions=False)
    a = nc.dram_tensor("a", [ntiles * 128, C], F32, kind="ExternalInput")
    aq = nc.dram_tensor("aq", [ntiles * 128, C], F32, kind="ExternalOutput")
    NB = C // 16
    with contextlib.ExitStack() as es:
        al = Alloc(nc, es)
        sc = Script()
        at = al.sb("at", [128, C])
        mx = al.sb("mx", [128, NB])
        rcl = al.sb("rcl", [128, NB])
        av = al.sb("av", [128, C])
        acc = al.sb("acc", [128, C])
        tg = al.sb("tg", [128, C])
        ob = al.sb("ob", [128, C])
        for it in range(ntiles):
            def y_ld(eng, emit, it=it):
                emit(eng.dma_start(out=at[:, :],
                                   in_=a[it * 128:(it + 1) * 128, :]), 16)
            sc.y(y_ld, 16)

            def s_abs(eng, emit):
                emit(eng.activation(av[:, :], at[:, :], AF.Abs))
            sc.s(s_abs)

            def v_q(eng, emit):
                blocked = at[:, :].rearrange("p (nb k) -> p nb k", k=16)
                eng.reduce_max(mx[:, :].rearrange("p (nb u) -> p nb u", u=1),
                               blocked, AX.X, apply_absolute_value=True)
                eng.tensor_scalar(mx[:, :], mx[:, :], 1.0 / 6.0, 1e-12,
                                  ALU.mult, ALU.max)
                _recip(eng, rcl[:, :], mx[:, :])
                rb = rcl[:, :].unsqueeze(2).broadcast_to([128, NB, 16])
                av3 = av[:, :].rearrange("p (nb k) -> p nb k", k=16)
                eng.tensor_mul(av3, av3, rb)
                eng.tensor_single_scalar(acc[:, :], av[:, :], MID[0], ALU.is_gt)
                eng.tensor_scalar_mul(acc[:, :], acc[:, :], GSTEP[0])
                for mid, gs in zip(MID[1:], GSTEP[1:]):
                    eng.tensor_single_scalar(tg[:, :], av[:, :], mid, ALU.is_gt)
                    eng.scalar_tensor_tensor(acc[:, :], tg[:, :], gs, acc[:, :],
                                             ALU.mult, ALU.add)
                mb = mx[:, :].unsqueeze(2).broadcast_to([128, NB, 16])
                ac3 = acc[:, :].rearrange("p (nb k) -> p nb k", k=16)
                eng.tensor_mul(ac3, ac3, mb)
                eng.tensor_single_scalar(tg[:, :], at[:, :], 0.0, ALU.is_ge)
                eng.tensor_scalar(tg[:, :], tg[:, :], 2.0, -1.0, ALU.mult, ALU.add)
                emit(eng.tensor_mul(ob[:, :], acc[:, :], tg[:, :]))
            sc.v(v_q)

            def g_st(eng, emit, it=it):
                emit(eng.dma_start(out=aq[it * 128:(it + 1) * 128, :],
                                   in_=ob[:, :]), 16)
            sc.y(g_st, 16)
        emit_script(nc, es, sc)
    return nc



def build_fp4gu():
    nc = bass.Bass("TRN2", target_bir_lowering=False,
                   detect_race_conditions=False)
    xq = nc.dram_tensor("xq", [D, NT], F32, kind="ExternalInput")
    wgq = nc.dram_tensor("wgq", [D, FFI], F32, kind="ExternalInput")
    wuq = nc.dram_tensor("wuq", [D, FFI], F32, kind="ExternalInput")
    h4 = nc.dram_tensor("h4", [FFI, NT], F32, kind="ExternalOutput")
    with contextlib.ExitStack() as es:
        al = Alloc(nc, es)
        sc = Script()
        xt = al.sb("xt", [128, PC * NT])
        wgt = al.sb("wgt", [128, PC * 128])
        wut = al.sb("wut", [128, PC * 128])
        tt = al.sb("tt", [128, 512])
        u1 = al.sb("u1", [128, 512])
        stg = al.sb("stg", [128, NT])
        psG = al.ps("psG", [128, 512])
        psY = al.ps("psY", [128, 512])
        add_load_fm(sc, xt, xq, NT, PC)
        for nj in range(FFI // 128):
            def y_w(eng, emit, nj=nj):
                for kc in range(PC):
                    emit(eng.dma_start(
                        out=wgt[:, kc * 128:(kc + 1) * 128],
                        in_=wgq[kc * 128:(kc + 1) * 128, nj * 128:(nj + 1) * 128]), 16)
                    emit(eng.dma_start(
                        out=wut[:, kc * 128:(kc + 1) * 128],
                        in_=wuq[kc * 128:(kc + 1) * 128, nj * 128:(nj + 1) * 128]), 16)
            sc.y(y_w, 16 * 2 * PC)
            for c0, w in _chunks(NT):
                def t_mm(eng, emit, c0=c0, w=w):
                    for kc in range(PC):
                        eng.matmul(psG[:, 0:w], wgt[:, kc * 128:(kc + 1) * 128],
                                   xt[:, kc * NT + c0:kc * NT + c0 + w],
                                   start=(kc == 0), stop=(kc == PC - 1))
                    for kc in range(PC):
                        i = eng.matmul(psY[:, 0:w], wut[:, kc * 128:(kc + 1) * 128],
                                       xt[:, kc * NT + c0:kc * NT + c0 + w],
                                       start=(kc == 0), stop=(kc == PC - 1))
                    emit(i)
                sc.t(t_mm)

                def s_th(eng, emit, c0=c0, w=w):
                    emit(eng.activation(tt[:, 0:w], psG[:, 0:w], AF.Tanh, scale=0.5))
                sc.s(s_th)

                def v_si(eng, emit, c0=c0, w=w):
                    eng.tensor_mul(u1[:, 0:w], tt[:, 0:w], psG[:, 0:w])
                    eng.tensor_add(u1[:, 0:w], u1[:, 0:w], psG[:, 0:w])
                    eng.tensor_scalar_mul(u1[:, 0:w], u1[:, 0:w], 0.5)
                    emit(eng.tensor_mul(stg[:, c0:c0 + w], u1[:, 0:w], psY[:, 0:w]))
                sc.v(v_si)

            def g_st(eng, emit, nj=nj):
                emit(eng.dma_start(out=h4[nj * 128:(nj + 1) * 128, :],
                                   in_=stg[:, :]), 16)
            sc.y(g_st, 16)
        emit_script(nc, es, sc)
    return nc



def build_fp4dn():
    nc = bass.Bass("TRN2", target_bir_lowering=False,
                   detect_race_conditions=False)
    hq = nc.dram_tensor("hq", [FFI, NT], F32, kind="ExternalInput")
    wdq = nc.dram_tensor("wdq", [FFI, D], F32, kind="ExternalInput")
    g2 = nc.dram_tensor("g2", [D, NT], F32, kind="ExternalOutput")
    KCH = FFI // 128
    with contextlib.ExitStack() as es:
        al = Alloc(nc, es)
        sc = Script()
        wdt = al.sb("wdt", [128, KCH * D])
        hqc = al.sb("hqc", [128, KCH * 512])
        tt = al.sb("tt", [128, 512])
        u1 = al.sb("u1", [128, 512])
        stg = al.sb("stg", [128, PC * 512])
        psD = al.ps("psD", [128, 512])

        def y_w(eng, emit):
            for kc in range(KCH):
                emit(eng.dma_start(out=wdt[:, kc * D:(kc + 1) * D],
                                   in_=wdq[kc * 128:(kc + 1) * 128, :]), 16)
        sc.y(y_w, 16 * KCH)
        for c0, w in _chunks(NT):
            def y_h(eng, emit, c0=c0, w=w):
                for kc in range(KCH):
                    emit(eng.dma_start(out=hqc[:, kc * 512:kc * 512 + w],
                                       in_=hq[kc * 128:(kc + 1) * 128, c0:c0 + w]), 16)
            sc.y(y_h, 16 * KCH)
            for nj in range(PC):
                def t_mm(eng, emit, nj=nj, w=w):
                    for kc in range(KCH):
                        i = eng.matmul(psD[:, 0:w],
                                       wdt[:, kc * D + nj * 128:kc * D + (nj + 1) * 128],
                                       hqc[:, kc * 512:kc * 512 + w],
                                       start=(kc == 0), stop=(kc == KCH - 1))
                    emit(i)
                sc.t(t_mm)

                def s_th(eng, emit, w=w):
                    emit(eng.activation(tt[:, 0:w], psD[:, 0:w], AF.Tanh, scale=0.5))
                sc.s(s_th)

                def v_si(eng, emit, nj=nj, w=w):
                    eng.tensor_mul(u1[:, 0:w], tt[:, 0:w], psD[:, 0:w])
                    emit(eng.tensor_add(stg[:, nj * 512:nj * 512 + w],
                                        u1[:, 0:w], psD[:, 0:w]))
                sc.v(v_si)

            def g_st(eng, emit, c0=c0, w=w):
                for nj in range(PC):
                    emit(eng.dma_start(out=g2[nj * 128:(nj + 1) * 128, c0:c0 + w],
                                       in_=stg[:, nj * 512:nj * 512 + w]), 16)
            sc.y(g_st, 16 * PC)
        emit_script(nc, es, sc)
    return nc



def build_lncg():
    NTH = NT + 3
    nc = bass.Bass("TRN2", target_bir_lowering=False,
                   detect_race_conditions=False)
    xa = nc.dram_tensor("xa", [D, NTH], F32, kind="ExternalInput")
    ln1w = nc.dram_tensor("ln1w", [D], F32, kind="ExternalInput")
    convw = nc.dram_tensor("convw", [D, KC], F32, kind="ExternalInput")
    convb = nc.dram_tensor("convb", [D], F32, kind="ExternalInput")
    wstk = nc.dram_tensor("wstk", [H, 4, DH, DH], F32, kind="ExternalInput")
    cellb = nc.dram_tensor("cellb", [H, 4, DH], F32, kind="ExternalInput")
    wx = nc.dram_tensor("wx", [H * 1024, NT], F32, kind="ExternalOutput")
    with contextlib.ExitStack() as es:
        al = Alloc(nc, es)
        sc = Script()
        xat = al.sb("xat", [128, PC * NTH])
        xc = al.sb("xc", [128, PC * NT])
        lnw_s = al.sb("lnw_s", [128, PC])
        cw_s = al.sb("cw_s", [128, PC * KC])
        cb_s = al.sb("cb_s", [128, PC])
        w_s = al.sb("w_s", [128, H * 4 * 2 * DH])
        b_s = al.sb("b_s", [128, H * 4 * 2])
        wxst = al.sb("wxst", [128, NT])
        psG = al.ps("psG", [128, 512])
        ones, onerow = add_consts(sc, al)
        add_load_fm(sc, xat, xa, NTH, PC)

        def y_w(eng, emit):
            emit(eng.dma_start(out=lnw_s[:, :],
                               in_=ln1w[:].rearrange("(c p) -> p c", p=128),
                               allow_slow_non_contiguous=True), 16)
            emit(eng.dma_start(out=cw_s[:, :].rearrange("p (c k) -> p c k", k=KC),
                               in_=convw[:, :].rearrange("(c p) k -> p c k", p=128),
                               allow_slow_non_contiguous=True), 16)
            emit(eng.dma_start(out=cb_s[:, :],
                               in_=convb[:].rearrange("(c p) -> p c", p=128),
                               allow_slow_non_contiguous=True), 16)
            for h in range(H):
                emit(eng.dma_start(
                    out=w_s[:, h * 8 * DH:(h + 1) * 8 * DH].rearrange(
                        "p (g kh o) -> p g kh o", g=4, kh=2),
                    in_=wstk[h, :, :, :].rearrange("g (kh p) o -> p g kh o",
                                                   p=128)), 16)
            emit(eng.dma_start(out=b_s[:, :],
                               in_=cellb[:, :, :].rearrange(
                                   "h g (oh p) -> p (h g oh)", p=128),
                               allow_slow_non_contiguous=True), 16)
        sc.y(y_w, 16 * (4 + H))

        def v_scale(eng, emit):
            for h in range(H):
                base = h * 4 * 2 * DH
                eng.tensor_scalar_mul(w_s[:, base:base + 2 * DH],
                                      w_s[:, base:base + 2 * DH], 0.5)
                eng.tensor_scalar_mul(w_s[:, base + 2 * DH:base + 4 * DH],
                                      w_s[:, base + 2 * DH:base + 4 * DH], -2.0)
                eng.tensor_scalar_mul(w_s[:, base + 4 * DH:base + 6 * DH],
                                      w_s[:, base + 4 * DH:base + 6 * DH], -0.5)
                eng.tensor_scalar_mul(w_s[:, base + 6 * DH:base + 8 * DH],
                                      w_s[:, base + 6 * DH:base + 8 * DH], -1.0)
                bb = h * 8
                eng.tensor_scalar_mul(b_s[:, bb + 2:bb + 4],
                                      b_s[:, bb + 2:bb + 4], -2.0)
                i = eng.tensor_scalar_mul(b_s[:, bb + 4:bb + 8],
                                          b_s[:, bb + 4:bb + 8], -1.0)
            emit(i)
        sc.v(v_scale)
        sh = make_ln_shared(al, NTH)
        add_ln(sc, sh, xat, xat, NTH, PC, ones, onerow, w_sb=lnw_s)
        for pc in range(PC):
            def v_conv(eng, emit, pc=pc):
                bi = pc * NTH
                bo = pc * NT
                eng.tensor_scalar(xc[:, bo:bo + NT], xat[:, bi:bi + NT],
                                  cw_s[:, pc * KC:pc * KC + 1], cb_s[:, pc:pc + 1],
                                  ALU.mult, ALU.add)
                for k in range(1, KC):
                    i = eng.scalar_tensor_tensor(
                        xc[:, bo:bo + NT], xat[:, bi + k:bi + k + NT],
                        cw_s[:, pc * KC + k:pc * KC + k + 1], xc[:, bo:bo + NT],
                        ALU.mult, ALU.add)
                emit(i)
            sc.v(v_conv)

            def s_th(eng, emit, pc=pc):
                emit(eng.activation(wxst[:, :], xc[:, pc * NT:(pc + 1) * NT],
                                    AF.Tanh, scale=0.5))
            sc.s(s_th)

            def v_si(eng, emit, pc=pc):
                bo = pc * NT
                eng.tensor_mul(wxst[:, :], wxst[:, :], xc[:, bo:bo + NT])
                emit(eng.tensor_add(xc[:, bo:bo + NT], xc[:, bo:bo + NT],
                                    wxst[:, :]))
            sc.v(v_si)
        for h in range(H):
            for g in range(4):
                src = xc if g in (0, 2) else xat
                ncol = NT if g in (0, 2) else NTH
                off = 0 if g in (0, 2) else 3
                for oh in range(2):
                    for c0, w in _chunks(NT):
                        def t_mm(eng, emit, h=h, g=g, oh=oh, c0=c0, w=w,
                                 src=src, ncol=ncol, off=off):
                            for kh in range(2):
                                stat = w_s[:, ((h * 4 + g) * 2 + kh) * DH + oh * 128:
                                            ((h * 4 + g) * 2 + kh) * DH + oh * 128 + 128]
                                pcs = 2 * h + kh
                                i = eng.matmul(
                                    psG[:, 0:w], stat,
                                    src[:, pcs * ncol + off + c0:pcs * ncol + off + c0 + w],
                                    start=(kh == 0), stop=(kh == 1))
                            emit(i)
                        sc.t(t_mm)

                        def v_ev(eng, emit, h=h, g=g, oh=oh, c0=c0, w=w):
                            emit(eng.tensor_scalar(
                                wxst[:, c0:c0 + w], psG[:, 0:w],
                                b_s[:, (h * 4 + g) * 2 + oh:(h * 4 + g) * 2 + oh + 1],
                                None, ALU.add))
                        sc.v(v_ev)

                    def g_st(eng, emit, h=h, g=g, oh=oh):
                        row = h * 1024 + (g * 2 + oh) * 128
                        emit(eng.dma_start(out=wx[row:row + 128, :],
                                           in_=wxst[:, :]), 16)
                    sc.y(g_st, 16)
        emit_script(nc, es, sc)
    return nc



def build_scan(T, TCH, dbg=None):
    nc = bass.Bass("TRN2", target_bir_lowering=False,
                   detect_race_conditions=False)
    wxd = nc.dram_tensor("wx", [128, T * 16], F32, kind="ExternalInput")
    rd = nc.dram_tensor("r", [256, 1024], F32, kind="ExternalInput")
    gnwd = nc.dram_tensor("gnw", [128, 2], F32, kind="ExternalInput")
    sti = nc.dram_tensor("statei", [128, 16], F32, kind="ExternalInput")
    ygnd = nc.dram_tensor("ygn", [128, T * 4], F32, kind="ExternalOutput")
    sto = nc.dram_tensor("stateo", [128, 16], F32, kind="ExternalOutput")
    tapd = None
    if dbg == "tap":
        tapd = nc.dram_tensor("tap", [128, T * 96], F32, kind="ExternalOutput")
    NCH = T // TCH
    C4 = TCH * 4
    with contextlib.ExitStack() as es:
        al = Alloc(nc, es)
        sc = Script()
        r_s = al.sb("r_s", [128, 2048])
        wxb = al.sb("wxb", [128, TCH * 16])
        ysb = al.sb("ysb", [128, C4])
        sqg = al.sb("sqg", [128, C4])
        stb = al.sb("stb", [128, 16])
        gnw = al.sb("gnw_s", [128, 2])
        hc = al.sb("hc", [128, 4])
        cs = al.sb("cs", [128, 4])
        ns = al.sb("ns", [128, 4])
        Ps = al.sb("Ps", [128, 4])
        rawb = al.sb("rawb", [128, 16])
        eb16 = al.sb("eb16", [128, 16])
        scr = al.sb("scr", [128, 48])
        jk = al.sb("jk", [128, 2])
        gst = al.sb("gst", [1, 2 * C4])
        gtm = al.sb("gtm", [1, 2 * C4])
        gab = al.sb("gab", [1, 2 * C4])
        grow = al.sb("grow", [1, 2 * C4])
        ones = al.sb("ones_c", [128, 1])
        onerow = al.sb("onerow_c", [1, 128])
        ps = al.ps("ps", [128, 16])
        psS = al.ps("psS", [1, 512])
        psQ = al.ps("psQ", [1, 512])
        psA = al.ps("psA", [128, 512])
        psB = al.ps("psB", [128, 512])

        def y_init(eng, emit):
            emit(eng.dma_start(out=r_s[:, 0:1024], in_=rd[0:128, :]), 16)
            emit(eng.dma_start(out=r_s[:, 1024:2048], in_=rd[128:256, :]), 16)
            emit(eng.dma_start(out=stb[:, :], in_=sti[:, :]), 16)
            emit(eng.dma_start(out=gnw[:, :], in_=gnwd[:, :]), 16)
        sc.y(y_init, 64)

        def v_init(eng, emit):
            eng.memset(ones[:, :], 1.0)
            eng.memset(onerow[:, :], 1.0)
            eng.tensor_scalar_mul(r_s[:, 256:512], r_s[:, 256:512], -2.0)
            eng.tensor_scalar_mul(r_s[:, 512:1024], r_s[:, 512:1024], -1.0)
            eng.tensor_scalar_mul(r_s[:, 1280:1536], r_s[:, 1280:1536], -2.0)
            eng.tensor_scalar_mul(r_s[:, 1536:2048], r_s[:, 1536:2048], -1.0)
            eng.tensor_copy(hc[:, :], stb[:, 0:4])
            eng.tensor_copy(cs[:, :], stb[:, 4:8])
            eng.tensor_copy(ns[:, :], stb[:, 8:12])
            emit(eng.tensor_copy(Ps[:, :], stb[:, 12:16]))
        sc.v(v_init)

        for ci in range(NCH):
            def y_wx(eng, emit, ci=ci):
                emit(eng.dma_start(out=wxb[:, :],
                                   in_=wxd[:, ci * TCH * 16:(ci + 1) * TCH * 16]), 16)
            sc.y(y_wx, 16)
            for t in range(TCH):
                def t_mv(eng, emit, t=t):
                    hsrc = hc if t == 0 else ysb[:, (t - 1) * 4:t * 4]
                    for j in range(8):
                        for kh in range(2):
                            i = eng.matmul(ps[:, j * 2:(j + 1) * 2],
                                           r_s[:, kh * 1024 + j * 128:kh * 1024 + (j + 1) * 128],
                                           hsrc[:, kh * 2:kh * 2 + 2],
                                           start=(j == 0 and kh == 0),
                                           stop=(j == 7 and kh == 1))
                    emit(i)
                sc.t(t_mv)

                def v_raw(eng, emit, t=t):
                    emit(eng.tensor_add(rawb[:, :], ps[:, :],
                                        wxb[:, t * 16:(t + 1) * 16]))
                sc.v(v_raw)

                def s_lut(eng, emit):
                    emit(eng.activation(eb16[:, :], rawb[:, :], AF.Exp))
                sc.s(s_lut)

                def v_cell(eng, emit, t=t):
                    if dbg == "eb":
                        emit(eng.tensor_copy(ysb[:, t * 4:(t + 1) * 4],
                                             eb16[:, 0:4]))
                        return
                    def phi():
                        eng.tensor_copy(jk[:, 0:1], jk[:, 1:2])
                    eng.tensor_scalar_add(scr[:, 36:48], eb16[:, 4:16], 1.0)
                    phi()
                    eng.reciprocal(scr[:, 0:12], scr[:, 36:48])
                    phi()
                    eng.tensor_mul(scr[:, 16:20], scr[:, 4:8], Ps[:, :])
                    eng.tensor_scalar(scr[:, 12:16], scr[:, 0:4], 2.0, -1.0,
                                      ALU.mult, ALU.add)
                    eng.tensor_max(Ps[:, :], eb16[:, 0:4], scr[:, 16:20])
                    phi()
                    eng.reciprocal(scr[:, 20:24], Ps[:, :])
                    phi()
                    eng.tensor_mul(scr[:, 24:28], eb16[:, 0:4], scr[:, 20:24])
                    eng.tensor_mul(scr[:, 28:32], scr[:, 16:20], scr[:, 20:24])
                    eng.tensor_mul(scr[:, 16:20], scr[:, 24:28], scr[:, 12:16])
                    eng.tensor_mul(cs[:, :], scr[:, 28:32], cs[:, :])
                    eng.tensor_mul(ns[:, :], scr[:, 28:32], ns[:, :])
                    eng.tensor_add(cs[:, :], cs[:, :], scr[:, 16:20])
                    eng.tensor_add(ns[:, :], ns[:, :], scr[:, 24:28])
                    eng.tensor_mul(scr[:, 16:20], scr[:, 8:12], cs[:, :])
                    eng.reciprocal(scr[:, 20:24], ns[:, :])
                    phi()
                    emit(eng.tensor_mul(ysb[:, t * 4:(t + 1) * 4],
                                        scr[:, 16:20], scr[:, 20:24]))
                sc.v(v_cell)
                if dbg == "tap":
                    def y_tap(eng, emit, t=t, ci=ci):
                        tg = (ci * TCH + t) * 96
                        emit(eng.dma_start(out=tapd[:, tg:tg + 16], in_=rawb[:, :]), 16)
                        emit(eng.dma_start(out=tapd[:, tg + 16:tg + 32], in_=eb16[:, :]), 16)
                        emit(eng.dma_start(out=tapd[:, tg + 32:tg + 64], in_=scr[:, :]), 16)
                        emit(eng.dma_start(out=tapd[:, tg + 64:tg + 68], in_=hc[:, :]), 16)
                        emit(eng.dma_start(out=tapd[:, tg + 68:tg + 72], in_=cs[:, :]), 16)
                        emit(eng.dma_start(out=tapd[:, tg + 72:tg + 76], in_=ns[:, :]), 16)
                        emit(eng.dma_start(out=tapd[:, tg + 76:tg + 80], in_=Ps[:, :]), 16)
                        emit(eng.dma_start(out=tapd[:, tg + 80:tg + 96],
                                           in_=wxb[:, t * 16:(t + 1) * 16]), 16)
                    sc.y(y_tap, 16 * 8)

            def v_hc(eng, emit):
                emit(eng.tensor_copy(hc[:, :], ysb[:, C4 - 4:C4]))
            sc.v(v_hc)

            if dbg in ("nogn", "ebnogn"):
                def g_yst0(eng, emit, ci=ci):
                    emit(eng.dma_start(out=ygnd[:, ci * C4:(ci + 1) * C4],
                                       in_=ysb[:, :]), 16)
                sc.y(g_yst0, 16)
                continue
            def s_gsq(eng, emit):
                emit(eng.activation(sqg[:, :], ysb[:, :], AF.Square))
            sc.s(s_gsq)
            for c0, w in _chunks(C4):
                def t_gst(eng, emit, c0=c0, w=w):
                    eng.matmul(psS[:, 0:w], ones[:, :], ysb[:, c0:c0 + w],
                               start=True, stop=True)
                    emit(eng.matmul(psQ[:, 0:w], ones[:, :], sqg[:, c0:c0 + w],
                                    start=True, stop=True))
                sc.t(t_gst)

                def v_gcp(eng, emit, c0=c0, w=w):
                    eng.tensor_copy(gst[:, c0:c0 + w], psS[:, 0:w])
                    emit(eng.tensor_copy(gst[:, C4 + c0:C4 + c0 + w], psQ[:, 0:w]))
                sc.v(v_gcp)

            def v_gnstat(eng, emit):
                sA = gst[:, 0:C4].rearrange("r (t h b) -> r t h b", h=2, b=2)
                suv = gtm[:, 0:TCH * 2].rearrange("r (t b) -> r t b", b=2)
                eng.tensor_add(suv, sA[:, :, 0, :], sA[:, :, 1, :])
                qA = gst[:, C4:2 * C4].rearrange("r (t h b) -> r t h b", h=2, b=2)
                quv = gtm[:, C4:C4 + TCH * 2].rearrange("r (t b) -> r t b", b=2)
                eng.tensor_add(quv, qA[:, :, 0, :], qA[:, :, 1, :])
                n2 = TCH * 2
                mu = gab[:, 0:n2]
                vv = gab[:, C4:C4 + n2]
                eng.tensor_scalar_mul(mu, gtm[:, 0:n2], 1.0 / DH)
                eng.tensor_scalar_mul(vv, gtm[:, C4:C4 + n2], 1.0 / DH)
                eng.tensor_mul(gtm[:, 0:n2], mu, mu)
                eng.tensor_sub(vv, vv, gtm[:, 0:n2])
                emit(eng.tensor_scalar_add(vv, vv, EPS))
            sc.v(v_gnstat)

            def s_gsqrt(eng, emit):
                n2 = TCH * 2
                emit(eng.activation(gtm[:, 0:n2], gab[:, C4:C4 + n2], AF.Ln))
            sc.s(s_gsqrt)

            def s_gexp(eng, emit):
                n2 = TCH * 2
                emit(eng.activation(gtm[:, 0:n2], gtm[:, 0:n2], AF.Exp, scale=-0.5))
            sc.s(s_gexp)

            def v_gnewt(eng, emit):
                n2 = TCH * 2
                vv = gab[:, C4:C4 + n2]
                rs = gtm[:, 0:n2]
                t0 = gtm[:, C4:C4 + n2]
                eng.tensor_mul(t0, rs, rs)
                eng.tensor_mul(t0, vv, t0)
                eng.tensor_scalar(t0, t0, -0.5, 1.5, ALU.mult, ALU.add)
                eng.tensor_mul(rs, rs, t0)
                eng.tensor_mul(gtm[:, C4:C4 + n2], gab[:, 0:n2], rs)
                eng.tensor_scalar_mul(gtm[:, C4:C4 + n2], gtm[:, C4:C4 + n2], -1.0)
                a2 = gtm[:, 0:n2].rearrange("r (t b) -> r t b", b=2)
                a2 = a2.unsqueeze(2).broadcast_to([1, TCH, 2, 2])
                b2 = gtm[:, C4:C4 + n2].rearrange("r (t b) -> r t b", b=2)
                b2 = b2.unsqueeze(2).broadcast_to([1, TCH, 2, 2])
                arow = grow[:, 0:C4].rearrange("r (t h b) -> r t h b", h=2, b=2)
                brow = grow[:, C4:2 * C4].rearrange("r (t h b) -> r t h b", h=2, b=2)
                eng.tensor_copy(arow, a2)
                emit(eng.tensor_copy(brow, b2))
            sc.v(v_gnewt)

            for c0, w in _chunks(C4):
                def t_gbc(eng, emit, c0=c0, w=w):
                    eng.matmul(psA[:, 0:w], onerow[:, :], grow[:, c0:c0 + w],
                               start=True, stop=True)
                    emit(eng.matmul(psB[:, 0:w], onerow[:, :],
                                    grow[:, C4 + c0:C4 + c0 + w],
                                    start=True, stop=True))
                sc.t(t_gbc)

                def v_gapp(eng, emit, c0=c0, w=w):
                    eng.tensor_mul(ysb[:, c0:c0 + w], ysb[:, c0:c0 + w], psA[:, 0:w])
                    eng.tensor_add(ysb[:, c0:c0 + w], ysb[:, c0:c0 + w], psB[:, 0:w])
                    for hh in range(2):
                        yv = ysb[:, c0:c0 + w].rearrange("p (t h b) -> p t h b",
                                                         h=2, b=2)[:, :, hh, :]
                        i = eng.tensor_single_scalar(yv, yv, gnw[:, hh:hh + 1],
                                                     ALU.mult)
                    emit(i)
                sc.v(v_gapp)

            def g_yst(eng, emit, ci=ci):
                emit(eng.dma_start(out=ygnd[:, ci * C4:(ci + 1) * C4],
                                   in_=ysb[:, :]), 16)
            sc.y(g_yst, 16)

        def v_fin(eng, emit):
            eng.tensor_copy(stb[:, 0:4], hc[:, :])
            eng.tensor_copy(stb[:, 4:8], cs[:, :])
            eng.tensor_copy(stb[:, 8:12], ns[:, :])
            emit(eng.tensor_copy(stb[:, 12:16], Ps[:, :]))
        sc.v(v_fin)

        def g_fin(eng, emit):
            emit(eng.dma_start(out=sto[:, :], in_=stb[:, :]), 16)
        sc.y(g_fin, 16)
        emit_script(nc, es, sc)
    return nc



def build_ffn():
    CCW = 256
    GV = [(j * 128, 128) for j in range(10)] + [(1280, 64)]
    KD = [(k * 128, 128) for k in range(10)] + [(1280, 64)]
    nc = bass.Bass("TRN2", target_bir_lowering=False,
                   detect_race_conditions=False)
    xa = nc.dram_tensor("xa", [D, NT], F32, kind="ExternalInput")
    yg = nc.dram_tensor("yg", [D, NT], F32, kind="ExternalInput")
    ln2w = nc.dram_tensor("ln2w", [D], F32, kind="ExternalInput")
    ffup = nc.dram_tensor("ffup", [D, 2 * UP], F32, kind="ExternalInput")
    ffdn = nc.dram_tensor("ffdn", [UP, D], F32, kind="ExternalInput")
    xo = nc.dram_tensor("xo", [D, NT], F32, kind="ExternalOutput")
    with contextlib.ExitStack() as es:
        al = Alloc(nc, es)
        sc = Script()
        fut = al.sb("fut", [128, PC * 2 * UP])
        fdt = al.sb("fdt", [128, 11 * D])
        lnw_s = al.sb("lnw_s", [128, PC])
        xac = al.sb("xac", [128, PC * CCW])
        xnc = al.sb("xnc", [128, PC * CCW])
        hidt = al.sb("hidt", [128, 11 * CCW])
        gj = al.sb("gj", [128, CCW])
        psU = al.ps("psU", [128, 512])
        psV = al.ps("psV", [128, 512])
        psD = al.ps("psD", [128, 512])
        ones, onerow = add_consts(sc, al)
        sh = make_ln_shared(al, CCW)

        def y_w(eng, emit):
            emit(eng.dma_start(out=lnw_s[:, :],
                               in_=ln2w[:].rearrange("(c p) -> p c", p=128),
                               allow_slow_non_contiguous=True), 16)
            for kc in range(PC):
                emit(eng.dma_start(out=fut[:, kc * 2 * UP:(kc + 1) * 2 * UP],
                                   in_=ffup[kc * 128:(kc + 1) * 128, :]), 16)
            for kc, (r0, kk) in enumerate(KD):
                emit(eng.dma_start(out=fdt[0:kk, kc * D:(kc + 1) * D],
                                   in_=ffdn[r0:r0 + kk, :]), 16)
        sc.y(y_w, 16 * (1 + PC + 11))

        for c0, w in _chunks(NT, CCW):
            def y_x(eng, emit, c0=c0, w=w):
                for pc in range(PC):
                    emit(eng.dma_start(out=xac[:, pc * CCW:pc * CCW + w],
                                       in_=xa[pc * 128:(pc + 1) * 128, c0:c0 + w]), 16)
                for pc in range(PC):
                    emit(eng.dma_start(out=xnc[:, pc * CCW:pc * CCW + w],
                                       in_=yg[pc * 128:(pc + 1) * 128, c0:c0 + w]), 16)
            sc.y(y_x, 16 * 2 * PC)

            def v_res(eng, emit, w=w):
                for pc in range(PC):
                    i = eng.tensor_add(xac[:, pc * CCW:pc * CCW + w],
                                       xac[:, pc * CCW:pc * CCW + w],
                                       xnc[:, pc * CCW:pc * CCW + w])
                emit(i)
            sc.v(v_res)
            add_ln(sc, sh, xac, xnc, w, PC, ones, onerow, w_sb=lnw_s)
            for j, (jb, M) in enumerate(GV):
                def t_up(eng, emit, jb=jb, M=M, w=w):
                    for kc in range(PC):
                        eng.matmul(psU[0:M, 0:w],
                                   fut[:, kc * 2 * UP + jb:kc * 2 * UP + jb + M],
                                   xnc[:, kc * CCW:kc * CCW + w],
                                   start=(kc == 0), stop=(kc == PC - 1))
                    for kc in range(PC):
                        i = eng.matmul(psV[0:M, 0:w],
                                       fut[:, kc * 2 * UP + UP + jb:kc * 2 * UP + UP + jb + M],
                                       xnc[:, kc * CCW:kc * CCW + w],
                                       start=(kc == 0), stop=(kc == PC - 1))
                    emit(i)
                sc.t(t_up)

                def s_gelu(eng, emit, j=j, M=M, w=w):
                    emit(eng.activation(gj[0:M, 0:w], psU[0:M, 0:w], AF.Gelu))
                sc.s(s_gelu)

                def v_hmul(eng, emit, j=j, M=M, w=w):
                    emit(eng.tensor_mul(hidt[0:M, j * CCW:j * CCW + w],
                                        gj[0:M, 0:w], psV[0:M, 0:w]))
                sc.v(v_hmul)
            for nj in range(PC):
                def t_dn(eng, emit, nj=nj, w=w):
                    for kc, (r0, kk) in enumerate(KD):
                        i = eng.matmul(psD[:, 0:w],
                                       fdt[0:kk, kc * D + nj * 128:kc * D + (nj + 1) * 128],
                                       hidt[0:kk, kc * CCW:kc * CCW + w],
                                       start=(kc == 0), stop=(kc == 10))
                    emit(i)
                sc.t(t_dn)

                def v_out(eng, emit, nj=nj, w=w):
                    emit(eng.tensor_add(xac[:, nj * CCW:nj * CCW + w],
                                        xac[:, nj * CCW:nj * CCW + w], psD[:, 0:w]))
                sc.v(v_out)

            def g_st(eng, emit, c0=c0, w=w):
                for pc in range(PC):
                    emit(eng.dma_start(out=xo[pc * 128:(pc + 1) * 128, c0:c0 + w],
                                       in_=xac[:, pc * CCW:pc * CCW + w]), 16)
            sc.y(g_st, 16 * PC)
        emit_script(nc, es, sc)
    return nc



def build_final():
    CCW = 512
    nc = bass.Bass("TRN2", target_bir_lowering=False,
                   detect_race_conditions=False)
    xa4 = nc.dram_tensor("xa4", [D, NT], F32, kind="ExternalInput")
    postw = nc.dram_tensor("postw", [D], F32, kind="ExternalInput")
    g2d = nc.dram_tensor("g2", [D, NT], F32, kind="ExternalInput")
    xind = nc.dram_tensor("xin", [D, NT], F32, kind="ExternalInput")
    outd = nc.dram_tensor("out", [D, NT], F32, kind="ExternalOutput")
    with contextlib.ExitStack() as es:
        al = Alloc(nc, es)
        sc = Script()
        xt = al.sb("xt", [128, PC * CCW])
        st = al.sb("st", [128, PC * CCW])
        g2t = al.sb("g2t", [128, PC * CCW])
        xit = al.sb("xit", [128, PC * CCW])
        pw_s = al.sb("pw_s", [128, PC])
        ones, onerow = add_consts(sc, al)
        sh = make_ln_shared(al, CCW)

        def y_w(eng, emit):
            emit(eng.dma_start(out=pw_s[:, :],
                               in_=postw[:].rearrange("(c p) -> p c", p=128),
                               allow_slow_non_contiguous=True), 16)
        sc.y(y_w, 16)
        for c0, w in _chunks(NT, CCW):
            def y_x(eng, emit, c0=c0, w=w):
                for pc in range(PC):
                    emit(eng.dma_start(out=xt[:, pc * CCW:pc * CCW + w],
                                       in_=xa4[pc * 128:(pc + 1) * 128, c0:c0 + w]), 16)
                    emit(eng.dma_start(out=g2t[:, pc * CCW:pc * CCW + w],
                                       in_=g2d[pc * 128:(pc + 1) * 128, c0:c0 + w]), 16)
                    emit(eng.dma_start(out=xit[:, pc * CCW:pc * CCW + w],
                                       in_=xind[pc * 128:(pc + 1) * 128, c0:c0 + w]), 16)
            sc.y(y_x, 16 * 3 * PC)
            add_ln(sc, sh, xt, st, w, PC, ones, onerow, w_sb=pw_s)

            def v_comb(eng, emit, w=w):
                for pc in range(PC):
                    sl = slice(pc * CCW, pc * CCW + w)
                    eng.scalar_tensor_tensor(st[:, sl], st[:, sl], 0.5, g2t[:, sl],
                                             ALU.mult, ALU.mult)
                    i = eng.tensor_add(st[:, sl], st[:, sl], xit[:, sl])
                emit(i)
            sc.v(v_comb)

            def g_st(eng, emit, c0=c0, w=w):
                for pc in range(PC):
                    emit(eng.dma_start(out=outd[pc * 128:(pc + 1) * 128, c0:c0 + w],
                                       in_=st[:, pc * CCW:pc * CCW + w]), 16)
            sc.y(g_st, 16 * PC)
        emit_script(nc, es, sc)
    return nc



_PROGS = {}


def _get(key, builder, *a):
    if key not in _PROGS:
        _PROGS[key] = builder(*a)
    return _PROGS[key]


def _run(nc, in_maps):
    r = run_bass_kernel_spmd(nc, in_maps, core_ids=list(range(NCORES)))
    return r.results


def _tok_slices(arr):
    return [np.ascontiguousarray(arr[c // 2, (c % 2) * NT:(c % 2 + 1) * NT, :].T)
            for c in range(NCORES)]


def _tok_unslices(slices, C=D):
    out = np.empty((B, S, C), np.float32)
    for c in range(NCORES):
        out[c // 2, (c % 2) * NT:(c % 2 + 1) * NT, :] = slices[c].T
    return out


def kernel(L0_attn_x, L0_attn_ln1_w, L0_attn_conv_w, L0_attn_conv_b,
           L0_attn_wi, L0_attn_wf, L0_attn_wz, L0_attn_wo, L0_attn_rker,
           L0_attn_cell_b, L0_attn_gn_w, L0_attn_ln2_w, L0_attn_ff_up,
           L0_attn_ff_down, L0_attn_post_w, L0_ffn_w_gate, L0_ffn_w_up,
           L0_ffn_w_down):
    f32 = lambda a: np.ascontiguousarray(np.asarray(a), dtype=np.float32)
    x = f32(L0_attn_x)
    ln1_w, conv_w, conv_b = f32(L0_attn_ln1_w), f32(L0_attn_conv_w), f32(L0_attn_conv_b)
    wi, wf, wz, wo = f32(L0_attn_wi), f32(L0_attn_wf), f32(L0_attn_wz), f32(L0_attn_wo)
    rker, cell_b = f32(L0_attn_rker), f32(L0_attn_cell_b)
    gn_w, ln2_w = f32(L0_attn_gn_w), f32(L0_attn_ln2_w)
    ff_up, ff_down, post_w = f32(L0_attn_ff_up), f32(L0_attn_ff_down), f32(L0_attn_post_w)
    w_gate, w_up, w_down = f32(L0_ffn_w_gate), f32(L0_ffn_w_up), f32(L0_ffn_w_down)

    xs = _tok_slices(x)
    res = _run(_get("prep", build_prep), [{"x": s} for s in xs])
    xn = _tok_unslices([r["xn"] for r in res])

    wgp = np.zeros((3072, D), np.float32)
    wgp[:FFI] = w_gate
    wup = np.zeros((3072, D), np.float32)
    wup[:FFI] = w_up
    nc_q3 = _get("tmq3_1024", build_tmq, 3, 1024)
    res = _run(nc_q3, [{"a": np.ascontiguousarray(wgp[c * 384:(c + 1) * 384])}
                       for c in range(NCORES)])
    wgq = np.concatenate([r["aq"] for r in res], 0)[:FFI]
    res = _run(nc_q3, [{"a": np.ascontiguousarray(wup[c * 384:(c + 1) * 384])}
                       for c in range(NCORES)])
    wuq = np.concatenate([r["aq"] for r in res], 0)[:FFI]
    nc_q1 = _get("tmq1_2816", build_tmq, 1, FFI)
    res = _run(nc_q1, [{"a": np.ascontiguousarray(w_down[c * 128:(c + 1) * 128])}
                       for c in range(NCORES)])
    wdq = np.concatenate([r["aq"] for r in res], 0)

    nc_qx = _get("tmq16_1024", build_tmq, 16, 1024)
    res = _run(nc_qx, [{"a": np.ascontiguousarray(xn[c // 2, (c % 2) * NT:(c % 2 + 1) * NT, :])}
                       for c in range(NCORES)])
    xq = _tok_unslices([np.ascontiguousarray(r["aq"].T) for r in res])
    wgqT = np.ascontiguousarray(wgq.T)
    wuqT = np.ascontiguousarray(wuq.T)
    res = _run(_get("fp4gu", build_fp4gu),
               [{"xq": s, "wgq": wgqT, "wuq": wuqT} for s in _tok_slices(xq)])
    h_full = _tok_unslices([r["h4"] for r in res], C=FFI)
    nc_qh = _get("tmq16_2816", build_tmq, 16, FFI)
    res = _run(nc_qh, [{"a": np.ascontiguousarray(h_full[c // 2, (c % 2) * NT:(c % 2 + 1) * NT, :])}
                       for c in range(NCORES)])
    hq = _tok_unslices([np.ascontiguousarray(r["aq"].T) for r in res], C=FFI)
    wdqT = np.ascontiguousarray(wdq.T)
    res = _run(_get("fp4dn", build_fp4dn),
               [{"hq": s, "wdq": wdqT} for s in _tok_slices(hq)])
    g2_sl = [r["g2"] for r in res]

    xa = xn
    T2 = S // 2
    for l in range(4):
        wstk = np.stack([wi[l].transpose(0, 2, 1), wz[l].transpose(0, 2, 1),
                         wf[l].transpose(0, 2, 1), wo[l].transpose(0, 2, 1)], 1)
        cellb_p = np.ascontiguousarray(cell_b[l][:, [0, 2, 1, 3], :])
        in_maps = []
        for c in range(NCORES):
            b, th = c // 2, c % 2
            t0 = th * NT
            halo = (np.zeros((3, D), np.float32) if th == 0
                    else xa[b, t0 - 3:t0, :])
            xasl = np.ascontiguousarray(
                np.concatenate([halo, xa[b, t0:t0 + NT, :]], 0).T)
            in_maps.append({"xa": xasl, "ln1w": ln1_w[l], "convw": conv_w[l],
                            "convb": conv_b[l], "wstk": np.ascontiguousarray(wstk),
                            "cellb": cellb_p})
        res = _run(_get("lncg", build_lncg), in_maps)
        wx_full = np.empty((H, 8, 128, B, S), np.float32)
        for c in range(NCORES):
            b, th = c // 2, c % 2
            wx_full[:, :, :, b, th * NT:(th + 1) * NT] = \
                res[c]["wx"].reshape(H, 8, 128, NT)
        nc_scan = _get("scan", build_scan, T2, 512)
        wx_sc, r_sc, gnw_sc = [], [], []
        for c in range(NCORES):
            h, bp = c // 2, c % 2
            arr = wx_full[h, :, :, 2 * bp:2 * bp + 2, :]
            wx_sc.append(np.ascontiguousarray(
                arr.transpose(1, 3, 0, 2).reshape(128, S * 16)))
            r_sc.append(np.ascontiguousarray(
                rker[l, h][:, [0, 2, 1, 3], :].reshape(DH, 1024)))
            gnw_sc.append(np.ascontiguousarray(
                gn_w[l, h * DH:(h + 1) * DH].reshape(2, 128).T))
        st0 = np.zeros((128, 16), np.float32)
        st0[:, 12:16] = 1.0
        res1 = _run(nc_scan, [{"wx": wx_sc[c][:, :T2 * 16], "r": r_sc[c],
                               "gnw": gnw_sc[c], "statei": st0}
                              for c in range(NCORES)])
        res2 = _run(nc_scan, [{"wx": np.ascontiguousarray(wx_sc[c][:, T2 * 16:]),
                               "r": r_sc[c], "gnw": gnw_sc[c],
                               "statei": res1[c]["stateo"]}
                              for c in range(NCORES)])
        y_full = np.empty((B, S, D), np.float32)
        for c in range(NCORES):
            h, bp = c // 2, c % 2
            for part, resx in ((0, res1), (1, res2)):
                yg = resx[c]["ygn"].reshape(128, T2, 2, 2)
                y_full[2 * bp:2 * bp + 2, part * T2:(part + 1) * T2,
                       h * DH:(h + 1) * DH] = \
                    yg.transpose(3, 1, 2, 0).reshape(2, T2, DH)
        res = _run(_get("ffn", build_ffn),
                   [{"xa": sa, "yg": sy, "ln2w": ln2_w[l], "ffup": ff_up[l],
                     "ffdn": ff_down[l]}
                    for sa, sy in zip(_tok_slices(xa), _tok_slices(y_full))])
        xa = _tok_unslices([r["xo"] for r in res])

    res = _run(_get("final", build_final),
               [{"xa4": sa, "postw": post_w, "g2": g2_sl[c], "xin": sx}
                for c, (sa, sx) in enumerate(zip(_tok_slices(xa), _tok_slices(x)))])
    out = _tok_unslices([r["out"] for r in res])
    return out


# version 05
# speedup vs baseline: 1.0003x; 1.0003x over previous
import contextlib
import numpy as np
import concourse.bass as bass
import concourse.mybir as mybir
from concourse.bass_utils import run_bass_kernel_spmd

F32 = mybir.dt.float32
BF16 = mybir.dt.bfloat16
AF = mybir.ActivationFunctionType
ALU = mybir.AluOpType
AX = mybir.AxisListType

B, S, D, H, DH, KC = 4, 4096, 1024, 4, 256, 4
UP = 1344
FFI = 2816
EPS = 1e-5
NCORES = 8
NT = 2048
PC = D // 128
MID = [0.25, 0.75, 1.25, 1.75, 2.5, 3.5, 5.0]
GSTEP = [0.5, 0.5, 0.5, 0.5, 1.0, 1.0, 2.0]



class Script:

    def __init__(self):
        self.steps = []

    def add(self, eng, fn, w=1, dma=False):
        self.steps.append((eng, fn, w, dma))

    def t(self, fn):
        self.add("tensor", fn, 1)

    def v(self, fn):
        self.add("vector", fn, 1)

    def s(self, fn):
        self.add("scalar", fn, 1)

    def g(self, fn):
        self.add("gpsimd", fn, 1)

    def y(self, fn, w=16):
        self.add("sync", fn, w, dma=True)


def emit_script(nc, es, sc):
    sem_c = es.enter_context(nc.semaphore("SCRIPT_C"))
    sem_d = es.enter_context(nc.semaphore("SCRIPT_D"))
    n = len(sc.steps)
    cumc = [0] * (n + 1)
    cumd = [0] * (n + 1)
    for i, (_, _, w, dma) in enumerate(sc.steps):
        cumc[i + 1] = cumc[i] + (0 if dma else w)
        cumd[i + 1] = cumd[i] + (w if dma else 0)
    engs = []
    for engname in ("sync", "tensor", "vector", "scalar", "gpsimd"):
        if any(e == engname for e, _, _, _ in sc.steps):
            engs.append(engname)
    with nc.Block() as block:
        for engname in engs:
            def make(engname=engname):
                def closure(eng):
                    prev = None
                    for i, (en, fn, w, dma) in enumerate(sc.steps):
                        if en != engname:
                            continue
                        if cumc[i] > 0 and prev != cumc[i]:
                            eng.wait_ge(sem_c, cumc[i])
                        if cumd[i] > 0:
                            eng.wait_ge(sem_d, cumd[i])
                        added = [0]

                        def emit(inst, k=None, added=added, dma=dma):
                            k = (16 if dma else 1) if k is None else k
                            if dma:
                                inst.then_inc(sem_d, k)
                            added[0] += k
                        fn(eng, emit)
                        assert added[0] == w, (
                            "step %d weight %d != emitted %d" % (i, w, added[0]))
                        if not dma:
                            eng.maybe_drain_then_inc((sem_c, w), fusable=True)
                        prev = cumc[i + 1]
                    eng.wait_ge(sem_c, cumc[n])
                    eng.wait_ge(sem_d, cumd[n])
                return closure
            getattr(block, engname)(make())


class Alloc:
    def __init__(self, nc, es):
        self.nc = nc
        self.es = es

    def sb(self, name, shape, dtype=F32):
        return self.es.enter_context(self.nc.sbuf_tensor(name, shape, dtype))

    def ps(self, name, shape, dtype=F32):
        return self.es.enter_context(self.nc.psum_tensor(name, shape, dtype))


def _recip(eng, out, in_):
    eng.drain()
    eng.reciprocal(out, in_)
    eng.drain()


def _chunks(total, step=512):
    out, c = [], 0
    while c < total:
        out.append((c, min(step, total - c)))
        c += step
    return out



def make_ln_shared(al, maxcols):
    del maxcols
    sh = {}
    sh["sq"] = al.sb("lnsq", [128, 512])
    sh["st"] = al.sb("lnst", [1, 2 * 512])
    sh["tm"] = al.sb("lntm", [1, 2 * 512])
    sh["ab"] = al.sb("lnab", [1, 2 * 512])
    sh["psS"] = al.ps("lnpsS", [1, 512])
    sh["psQ"] = al.ps("lnpsQ", [1, 512])
    sh["psA"] = al.ps("lnpsA", [128, 512])
    sh["psB"] = al.ps("lnpsB", [128, 512])
    return sh


def add_ln(sc, sh, xt, ot, ncols, PCn, ones, onerow, w_sb=None, nfeat=None):
    nfeat = nfeat or (PCn * 128)
    sq, st, tm, ab = sh["sq"], sh["st"], sh["tm"], sh["ab"]
    psS, psQ, psA, psB = sh["psS"], sh["psQ"], sh["psA"], sh["psB"]

    for c0, w in _chunks(ncols):
        for pc in range(PCn):
            def s_sq(eng, emit, pc=pc, c0=c0, w=w):
                emit(eng.activation(sq[:, 0:w],
                                    xt[:, pc * ncols + c0:pc * ncols + c0 + w],
                                    AF.Square))
            sc.s(s_sq)

            def t_mm(eng, emit, pc=pc, c0=c0, w=w):
                eng.matmul(psS[:, 0:w], ones[:, :],
                           xt[:, pc * ncols + c0:pc * ncols + c0 + w],
                           start=(pc == 0), stop=(pc == PCn - 1))
                emit(eng.matmul(psQ[:, 0:w], ones[:, :], sq[:, 0:w],
                                start=(pc == 0), stop=(pc == PCn - 1)))
            sc.t(t_mm)

        def v_stats(eng, emit, w=w):
            mu = tm[:, 0:w]
            vv = tm[:, 512:512 + w]
            eng.tensor_scalar_mul(mu, psS[:, 0:w], 1.0 / nfeat)
            eng.tensor_scalar_mul(vv, psQ[:, 0:w], 1.0 / nfeat)
            eng.tensor_mul(st[:, 0:w], mu, mu)
            eng.tensor_sub(vv, vv, st[:, 0:w])
            emit(eng.tensor_scalar_add(vv, vv, EPS))
        sc.v(v_stats)

        def s_sqrt(eng, emit, w=w):
            emit(eng.activation(ab[:, 0:w], tm[:, 512:512 + w], AF.Sqrt))
        sc.s(s_sqrt)

        def v_newton(eng, emit, w=w):
            vv = tm[:, 512:512 + w]
            s0 = ab[:, 0:w]
            t0 = st[:, 512:512 + w]
            for _ in range(2):
                _recip(eng, t0, s0)
                eng.tensor_mul(t0, vv, t0)
                eng.tensor_add(t0, t0, s0)
                eng.tensor_scalar_mul(s0, t0, 0.5)
            _recip(eng, ab[:, 0:w], s0)
            eng.tensor_mul(ab[:, 512:512 + w], tm[:, 0:w], ab[:, 0:w])
            emit(eng.tensor_scalar_mul(ab[:, 512:512 + w],
                                       ab[:, 512:512 + w], -1.0))
        sc.v(v_newton)

        def t_bc(eng, emit, c0=c0, w=w):
            eng.matmul(psA[:, 0:w], onerow[:, :], ab[:, 0:w],
                       start=True, stop=True)
            emit(eng.matmul(psB[:, 0:w], onerow[:, :], ab[:, 512:512 + w],
                            start=True, stop=True))
        sc.t(t_bc)

        def v_app(eng, emit, c0=c0, w=w):
            for pc in range(PCn):
                sl = slice(pc * ncols + c0, pc * ncols + c0 + w)
                eng.tensor_mul(ot[:, sl], xt[:, sl], psA[:, 0:w])
                i = eng.tensor_add(ot[:, sl], ot[:, sl], psB[:, 0:w])
                if w_sb is not None:
                    i = eng.tensor_scalar_mul(ot[:, sl], ot[:, sl],
                                              w_sb[:, pc:pc + 1])
            emit(i)
        sc.v(v_app)


def add_consts(sc, al):
    ones = al.sb("ones_c", [128, 1])
    onerow = al.sb("onerow_c", [1, 128])

    def v_init(eng, emit):
        eng.memset(ones[:, :], 1.0)
        emit(eng.memset(onerow[:, :], 1.0))
    sc.v(v_init)
    return ones, onerow


def add_load_fm(sc, dst, src, ncols, PCn):
    def y_ld(eng, emit):
        for pc in range(PCn):
            emit(eng.dma_start(out=dst[:, pc * ncols:(pc + 1) * ncols],
                               in_=src[pc * 128:(pc + 1) * 128, :]), 16)
    sc.y(y_ld, 16 * PCn)


def add_store_fm(sc, dst, src_sb, ncols, PCn):
    def g_st(eng, emit):
        for pc in range(PCn):
            emit(eng.dma_start(out=dst[pc * 128:(pc + 1) * 128, :],
                               in_=src_sb[:, pc * ncols:(pc + 1) * ncols]), 16)
    sc.y(g_st, 16 * PCn)



def build_prep():
    nc = bass.Bass("TRN2", target_bir_lowering=False,
                   detect_race_conditions=False)
    x = nc.dram_tensor("x", [D, NT], F32, kind="ExternalInput")
    xn = nc.dram_tensor("xn", [D, NT], F32, kind="ExternalOutput")
    with contextlib.ExitStack() as es:
        al = Alloc(nc, es)
        sc = Script()
        xt = al.sb("xt", [128, PC * NT])
        ones, onerow = add_consts(sc, al)
        add_load_fm(sc, xt, x, NT, PC)
        sh = make_ln_shared(al, NT)
        add_ln(sc, sh, xt, xt, NT, PC, ones, onerow)
        add_store_fm(sc, xn, xt, NT, PC)
        emit_script(nc, es, sc)
    return nc



def build_tmq(ntiles, C):
    nc = bass.Bass("TRN2", target_bir_lowering=False,
                   detect_race_conditions=False)
    a = nc.dram_tensor("a", [ntiles * 128, C], F32, kind="ExternalInput")
    aq = nc.dram_tensor("aq", [ntiles * 128, C], F32, kind="ExternalOutput")
    NB = C // 16
    with contextlib.ExitStack() as es:
        al = Alloc(nc, es)
        sc = Script()
        at = al.sb("at", [128, C])
        mx = al.sb("mx", [128, NB])
        rcl = al.sb("rcl", [128, NB])
        av = al.sb("av", [128, C])
        acc = al.sb("acc", [128, C])
        tg = al.sb("tg", [128, C])
        ob = al.sb("ob", [128, C])
        for it in range(ntiles):
            def y_ld(eng, emit, it=it):
                emit(eng.dma_start(out=at[:, :],
                                   in_=a[it * 128:(it + 1) * 128, :]), 16)
            sc.y(y_ld, 16)

            def s_abs(eng, emit):
                emit(eng.activation(av[:, :], at[:, :], AF.Abs))
            sc.s(s_abs)

            def v_q(eng, emit):
                blocked = at[:, :].rearrange("p (nb k) -> p nb k", k=16)
                eng.reduce_max(mx[:, :].rearrange("p (nb u) -> p nb u", u=1),
                               blocked, AX.X, apply_absolute_value=True)
                eng.tensor_scalar(mx[:, :], mx[:, :], 1.0 / 6.0, 1e-12,
                                  ALU.mult, ALU.max)
                _recip(eng, rcl[:, :], mx[:, :])
                rb = rcl[:, :].unsqueeze(2).broadcast_to([128, NB, 16])
                av3 = av[:, :].rearrange("p (nb k) -> p nb k", k=16)
                eng.tensor_mul(av3, av3, rb)
                eng.tensor_single_scalar(acc[:, :], av[:, :], MID[0], ALU.is_gt)
                eng.tensor_scalar_mul(acc[:, :], acc[:, :], GSTEP[0])
                for mid, gs in zip(MID[1:], GSTEP[1:]):
                    eng.tensor_single_scalar(tg[:, :], av[:, :], mid, ALU.is_gt)
                    eng.scalar_tensor_tensor(acc[:, :], tg[:, :], gs, acc[:, :],
                                             ALU.mult, ALU.add)
                mb = mx[:, :].unsqueeze(2).broadcast_to([128, NB, 16])
                ac3 = acc[:, :].rearrange("p (nb k) -> p nb k", k=16)
                eng.tensor_mul(ac3, ac3, mb)
                eng.tensor_single_scalar(tg[:, :], at[:, :], 0.0, ALU.is_ge)
                eng.tensor_scalar(tg[:, :], tg[:, :], 2.0, -1.0, ALU.mult, ALU.add)
                emit(eng.tensor_mul(ob[:, :], acc[:, :], tg[:, :]))
            sc.v(v_q)

            def g_st(eng, emit, it=it):
                emit(eng.dma_start(out=aq[it * 128:(it + 1) * 128, :],
                                   in_=ob[:, :]), 16)
            sc.y(g_st, 16)
        emit_script(nc, es, sc)
    return nc



def build_fp4gu():
    nc = bass.Bass("TRN2", target_bir_lowering=False,
                   detect_race_conditions=False)
    xq = nc.dram_tensor("xq", [D, NT], F32, kind="ExternalInput")
    wgq = nc.dram_tensor("wgq", [D, FFI], F32, kind="ExternalInput")
    wuq = nc.dram_tensor("wuq", [D, FFI], F32, kind="ExternalInput")
    h4 = nc.dram_tensor("h4", [FFI, NT], F32, kind="ExternalOutput")
    with contextlib.ExitStack() as es:
        al = Alloc(nc, es)
        sc = Script()
        xt = al.sb("xt", [128, PC * NT])
        wgt = al.sb("wgt", [128, PC * 128])
        wut = al.sb("wut", [128, PC * 128])
        tt = al.sb("tt", [128, 512])
        u1 = al.sb("u1", [128, 512])
        stg = al.sb("stg", [128, NT])
        psG = al.ps("psG", [128, 512])
        psY = al.ps("psY", [128, 512])
        add_load_fm(sc, xt, xq, NT, PC)
        for nj in range(FFI // 128):
            def y_w(eng, emit, nj=nj):
                for kc in range(PC):
                    emit(eng.dma_start(
                        out=wgt[:, kc * 128:(kc + 1) * 128],
                        in_=wgq[kc * 128:(kc + 1) * 128, nj * 128:(nj + 1) * 128]), 16)
                    emit(eng.dma_start(
                        out=wut[:, kc * 128:(kc + 1) * 128],
                        in_=wuq[kc * 128:(kc + 1) * 128, nj * 128:(nj + 1) * 128]), 16)
            sc.y(y_w, 16 * 2 * PC)
            for c0, w in _chunks(NT):
                def t_mm(eng, emit, c0=c0, w=w):
                    for kc in range(PC):
                        eng.matmul(psG[:, 0:w], wgt[:, kc * 128:(kc + 1) * 128],
                                   xt[:, kc * NT + c0:kc * NT + c0 + w],
                                   start=(kc == 0), stop=(kc == PC - 1))
                    for kc in range(PC):
                        i = eng.matmul(psY[:, 0:w], wut[:, kc * 128:(kc + 1) * 128],
                                       xt[:, kc * NT + c0:kc * NT + c0 + w],
                                       start=(kc == 0), stop=(kc == PC - 1))
                    emit(i)
                sc.t(t_mm)

                def s_th(eng, emit, c0=c0, w=w):
                    emit(eng.activation(tt[:, 0:w], psG[:, 0:w], AF.Tanh, scale=0.5))
                sc.s(s_th)

                def v_si(eng, emit, c0=c0, w=w):
                    eng.tensor_mul(u1[:, 0:w], tt[:, 0:w], psG[:, 0:w])
                    eng.tensor_add(u1[:, 0:w], u1[:, 0:w], psG[:, 0:w])
                    eng.tensor_scalar_mul(u1[:, 0:w], u1[:, 0:w], 0.5)
                    emit(eng.tensor_mul(stg[:, c0:c0 + w], u1[:, 0:w], psY[:, 0:w]))
                sc.v(v_si)

            def g_st(eng, emit, nj=nj):
                emit(eng.dma_start(out=h4[nj * 128:(nj + 1) * 128, :],
                                   in_=stg[:, :]), 16)
            sc.y(g_st, 16)
        emit_script(nc, es, sc)
    return nc



def build_fp4dn():
    nc = bass.Bass("TRN2", target_bir_lowering=False,
                   detect_race_conditions=False)
    hq = nc.dram_tensor("hq", [FFI, NT], F32, kind="ExternalInput")
    wdq = nc.dram_tensor("wdq", [FFI, D], F32, kind="ExternalInput")
    g2 = nc.dram_tensor("g2", [D, NT], F32, kind="ExternalOutput")
    KCH = FFI // 128
    with contextlib.ExitStack() as es:
        al = Alloc(nc, es)
        sc = Script()
        wdt = al.sb("wdt", [128, KCH * D])
        hqc = al.sb("hqc", [128, KCH * 512])
        tt = al.sb("tt", [128, 512])
        u1 = al.sb("u1", [128, 512])
        stg = al.sb("stg", [128, PC * 512])
        psD = al.ps("psD", [128, 512])

        def y_w(eng, emit):
            for kc in range(KCH):
                emit(eng.dma_start(out=wdt[:, kc * D:(kc + 1) * D],
                                   in_=wdq[kc * 128:(kc + 1) * 128, :]), 16)
        sc.y(y_w, 16 * KCH)
        for c0, w in _chunks(NT):
            def y_h(eng, emit, c0=c0, w=w):
                for kc in range(KCH):
                    emit(eng.dma_start(out=hqc[:, kc * 512:kc * 512 + w],
                                       in_=hq[kc * 128:(kc + 1) * 128, c0:c0 + w]), 16)
            sc.y(y_h, 16 * KCH)
            for nj in range(PC):
                def t_mm(eng, emit, nj=nj, w=w):
                    for kc in range(KCH):
                        i = eng.matmul(psD[:, 0:w],
                                       wdt[:, kc * D + nj * 128:kc * D + (nj + 1) * 128],
                                       hqc[:, kc * 512:kc * 512 + w],
                                       start=(kc == 0), stop=(kc == KCH - 1))
                    emit(i)
                sc.t(t_mm)

                def s_th(eng, emit, w=w):
                    emit(eng.activation(tt[:, 0:w], psD[:, 0:w], AF.Tanh, scale=0.5))
                sc.s(s_th)

                def v_si(eng, emit, nj=nj, w=w):
                    eng.tensor_mul(u1[:, 0:w], tt[:, 0:w], psD[:, 0:w])
                    emit(eng.tensor_add(stg[:, nj * 512:nj * 512 + w],
                                        u1[:, 0:w], psD[:, 0:w]))
                sc.v(v_si)

            def g_st(eng, emit, c0=c0, w=w):
                for nj in range(PC):
                    emit(eng.dma_start(out=g2[nj * 128:(nj + 1) * 128, c0:c0 + w],
                                       in_=stg[:, nj * 512:nj * 512 + w]), 16)
            sc.y(g_st, 16 * PC)
        emit_script(nc, es, sc)
    return nc



def build_lncg():
    NTH = NT + 3
    nc = bass.Bass("TRN2", target_bir_lowering=False,
                   detect_race_conditions=False)
    xa = nc.dram_tensor("xa", [D, NTH], F32, kind="ExternalInput")
    ln1w = nc.dram_tensor("ln1w", [D], F32, kind="ExternalInput")
    convw = nc.dram_tensor("convw", [D, KC], F32, kind="ExternalInput")
    convb = nc.dram_tensor("convb", [D], F32, kind="ExternalInput")
    wstk = nc.dram_tensor("wstk", [H, 4, DH, DH], F32, kind="ExternalInput")
    cellb = nc.dram_tensor("cellb", [H, 4, DH], F32, kind="ExternalInput")
    wx = nc.dram_tensor("wx", [H * 1024, NT], F32, kind="ExternalOutput")
    with contextlib.ExitStack() as es:
        al = Alloc(nc, es)
        sc = Script()
        xat = al.sb("xat", [128, PC * NTH])
        xc = al.sb("xc", [128, PC * NT])
        lnw_s = al.sb("lnw_s", [128, PC])
        cw_s = al.sb("cw_s", [128, PC * KC])
        cb_s = al.sb("cb_s", [128, PC])
        w_s = al.sb("w_s", [128, H * 4 * 2 * DH])
        b_s = al.sb("b_s", [128, H * 4 * 2])
        wxst = al.sb("wxst", [128, NT])
        psG = al.ps("psG", [128, 512])
        ones, onerow = add_consts(sc, al)
        add_load_fm(sc, xat, xa, NTH, PC)

        def y_w(eng, emit):
            emit(eng.dma_start(out=lnw_s[:, :],
                               in_=ln1w[:].rearrange("(c p) -> p c", p=128),
                               allow_slow_non_contiguous=True), 16)
            emit(eng.dma_start(out=cw_s[:, :].rearrange("p (c k) -> p c k", k=KC),
                               in_=convw[:, :].rearrange("(c p) k -> p c k", p=128),
                               allow_slow_non_contiguous=True), 16)
            emit(eng.dma_start(out=cb_s[:, :],
                               in_=convb[:].rearrange("(c p) -> p c", p=128),
                               allow_slow_non_contiguous=True), 16)
            for h in range(H):
                emit(eng.dma_start(
                    out=w_s[:, h * 8 * DH:(h + 1) * 8 * DH].rearrange(
                        "p (g kh o) -> p g kh o", g=4, kh=2),
                    in_=wstk[h, :, :, :].rearrange("g (kh p) o -> p g kh o",
                                                   p=128)), 16)
            emit(eng.dma_start(out=b_s[:, :],
                               in_=cellb[:, :, :].rearrange(
                                   "h g (oh p) -> p (h g oh)", p=128),
                               allow_slow_non_contiguous=True), 16)
        sc.y(y_w, 16 * (4 + H))

        def v_scale(eng, emit):
            for h in range(H):
                base = h * 4 * 2 * DH
                eng.tensor_scalar_mul(w_s[:, base:base + 2 * DH],
                                      w_s[:, base:base + 2 * DH], 0.5)
                eng.tensor_scalar_mul(w_s[:, base + 2 * DH:base + 4 * DH],
                                      w_s[:, base + 2 * DH:base + 4 * DH], -2.0)
                eng.tensor_scalar_mul(w_s[:, base + 4 * DH:base + 6 * DH],
                                      w_s[:, base + 4 * DH:base + 6 * DH], -0.5)
                eng.tensor_scalar_mul(w_s[:, base + 6 * DH:base + 8 * DH],
                                      w_s[:, base + 6 * DH:base + 8 * DH], -1.0)
                bb = h * 8
                eng.tensor_scalar_mul(b_s[:, bb + 2:bb + 4],
                                      b_s[:, bb + 2:bb + 4], -2.0)
                i = eng.tensor_scalar_mul(b_s[:, bb + 4:bb + 8],
                                          b_s[:, bb + 4:bb + 8], -1.0)
            emit(i)
        sc.v(v_scale)
        sh = make_ln_shared(al, NTH)
        add_ln(sc, sh, xat, xat, NTH, PC, ones, onerow, w_sb=lnw_s)
        for pc in range(PC):
            def v_conv(eng, emit, pc=pc):
                bi = pc * NTH
                bo = pc * NT
                eng.tensor_scalar(xc[:, bo:bo + NT], xat[:, bi:bi + NT],
                                  cw_s[:, pc * KC:pc * KC + 1], cb_s[:, pc:pc + 1],
                                  ALU.mult, ALU.add)
                for k in range(1, KC):
                    i = eng.scalar_tensor_tensor(
                        xc[:, bo:bo + NT], xat[:, bi + k:bi + k + NT],
                        cw_s[:, pc * KC + k:pc * KC + k + 1], xc[:, bo:bo + NT],
                        ALU.mult, ALU.add)
                emit(i)
            sc.v(v_conv)

            def s_th(eng, emit, pc=pc):
                emit(eng.activation(wxst[:, :], xc[:, pc * NT:(pc + 1) * NT],
                                    AF.Tanh, scale=0.5))
            sc.s(s_th)

            def v_si(eng, emit, pc=pc):
                bo = pc * NT
                eng.tensor_mul(wxst[:, :], wxst[:, :], xc[:, bo:bo + NT])
                emit(eng.tensor_add(xc[:, bo:bo + NT], xc[:, bo:bo + NT],
                                    wxst[:, :]))
            sc.v(v_si)
        for h in range(H):
            for g in range(4):
                src = xc if g in (0, 2) else xat
                ncol = NT if g in (0, 2) else NTH
                off = 0 if g in (0, 2) else 3
                for oh in range(2):
                    for c0, w in _chunks(NT):
                        def t_mm(eng, emit, h=h, g=g, oh=oh, c0=c0, w=w,
                                 src=src, ncol=ncol, off=off):
                            for kh in range(2):
                                stat = w_s[:, ((h * 4 + g) * 2 + kh) * DH + oh * 128:
                                            ((h * 4 + g) * 2 + kh) * DH + oh * 128 + 128]
                                pcs = 2 * h + kh
                                i = eng.matmul(
                                    psG[:, 0:w], stat,
                                    src[:, pcs * ncol + off + c0:pcs * ncol + off + c0 + w],
                                    start=(kh == 0), stop=(kh == 1))
                            emit(i)
                        sc.t(t_mm)

                        def v_ev(eng, emit, h=h, g=g, oh=oh, c0=c0, w=w):
                            emit(eng.tensor_scalar(
                                wxst[:, c0:c0 + w], psG[:, 0:w],
                                b_s[:, (h * 4 + g) * 2 + oh:(h * 4 + g) * 2 + oh + 1],
                                None, ALU.add))
                        sc.v(v_ev)

                    def g_st(eng, emit, h=h, g=g, oh=oh):
                        row = h * 1024 + (g * 2 + oh) * 128
                        emit(eng.dma_start(out=wx[row:row + 128, :],
                                           in_=wxst[:, :]), 16)
                    sc.y(g_st, 16)
        emit_script(nc, es, sc)
    return nc



def build_scan(T, TCH, dbg=None):
    nc = bass.Bass("TRN2", target_bir_lowering=False,
                   detect_race_conditions=False)
    wxd = nc.dram_tensor("wx", [128, T * 16], F32, kind="ExternalInput")
    rd = nc.dram_tensor("r", [256, 1024], F32, kind="ExternalInput")
    gnwd = nc.dram_tensor("gnw", [128, 2], F32, kind="ExternalInput")
    sti = nc.dram_tensor("statei", [128, 16], F32, kind="ExternalInput")
    ygnd = nc.dram_tensor("ygn", [128, T * 4], F32, kind="ExternalOutput")
    sto = nc.dram_tensor("stateo", [128, 16], F32, kind="ExternalOutput")
    tapd = None
    if dbg == "tap":
        tapd = nc.dram_tensor("tap", [128, T * 96], F32, kind="ExternalOutput")
    NCH = T // TCH
    C4 = TCH * 4
    with contextlib.ExitStack() as es:
        al = Alloc(nc, es)
        sc = Script()
        r_s = al.sb("r_s", [128, 2048])
        wxb = al.sb("wxb", [128, TCH * 16])
        ysb = al.sb("ysb", [128, C4])
        sqg = al.sb("sqg", [128, C4])
        stb = al.sb("stb", [128, 16])
        gnw = al.sb("gnw_s", [128, 2])
        hc = al.sb("hc", [128, 4])
        cs = al.sb("cs", [128, 4])
        ns = al.sb("ns", [128, 4])
        Ps = al.sb("Ps", [128, 4])
        rawb = al.sb("rawb", [128, 16])
        eb16 = al.sb("eb16", [128, 16])
        scr = al.sb("scr", [128, 48])
        jk = al.sb("jk", [128, 2])
        gst = al.sb("gst", [1, 2 * C4])
        gtm = al.sb("gtm", [1, 2 * C4])
        gab = al.sb("gab", [1, 2 * C4])
        grow = al.sb("grow", [1, 2 * C4])
        ones = al.sb("ones_c", [128, 1])
        onerow = al.sb("onerow_c", [1, 128])
        ps = al.ps("ps", [128, 16])
        psS = al.ps("psS", [1, 512])
        psQ = al.ps("psQ", [1, 512])
        psA = al.ps("psA", [128, 512])
        psB = al.ps("psB", [128, 512])

        def y_init(eng, emit):
            emit(eng.dma_start(out=r_s[:, 0:1024], in_=rd[0:128, :]), 16)
            emit(eng.dma_start(out=r_s[:, 1024:2048], in_=rd[128:256, :]), 16)
            emit(eng.dma_start(out=stb[:, :], in_=sti[:, :]), 16)
            emit(eng.dma_start(out=gnw[:, :], in_=gnwd[:, :]), 16)
        sc.y(y_init, 64)

        def v_init(eng, emit):
            eng.memset(ones[:, :], 1.0)
            eng.memset(onerow[:, :], 1.0)
            eng.memset(jk[:, :], 0.0)
            eng.tensor_scalar_mul(r_s[:, 256:512], r_s[:, 256:512], -2.0)
            eng.tensor_scalar_mul(r_s[:, 512:1024], r_s[:, 512:1024], -1.0)
            eng.tensor_scalar_mul(r_s[:, 1280:1536], r_s[:, 1280:1536], -2.0)
            eng.tensor_scalar_mul(r_s[:, 1536:2048], r_s[:, 1536:2048], -1.0)
            eng.tensor_copy(hc[:, :], stb[:, 0:4])
            eng.tensor_copy(cs[:, :], stb[:, 4:8])
            eng.tensor_copy(ns[:, :], stb[:, 8:12])
            emit(eng.tensor_copy(Ps[:, :], stb[:, 12:16]))
        sc.v(v_init)

        for ci in range(NCH):
            def y_wx(eng, emit, ci=ci):
                emit(eng.dma_start(out=wxb[:, :],
                                   in_=wxd[:, ci * TCH * 16:(ci + 1) * TCH * 16]), 16)
            sc.y(y_wx, 16)
            for t in range(TCH):
                def t_mv(eng, emit, t=t):
                    hsrc = hc if t == 0 else ysb[:, (t - 1) * 4:t * 4]
                    for j in range(8):
                        for kh in range(2):
                            i = eng.matmul(ps[:, j * 2:(j + 1) * 2],
                                           r_s[:, kh * 1024 + j * 128:kh * 1024 + (j + 1) * 128],
                                           hsrc[:, kh * 2:kh * 2 + 2],
                                           start=(j == 0 and kh == 0),
                                           stop=(j == 7 and kh == 1))
                    emit(i)
                sc.t(t_mv)

                def v_raw(eng, emit, t=t):
                    emit(eng.tensor_add(rawb[:, :], ps[:, :],
                                        wxb[:, t * 16:(t + 1) * 16]))
                sc.v(v_raw)

                def s_lut(eng, emit):
                    emit(eng.activation(eb16[:, :], rawb[:, :], AF.Exp))
                sc.s(s_lut)

                def v_cell(eng, emit, t=t):
                    if dbg == "eb":
                        emit(eng.tensor_copy(ysb[:, t * 4:(t + 1) * 4],
                                             eb16[:, 0:4]))
                        return
                    def phi():
                        eng.tensor_copy(jk[:, 0:1], jk[:, 1:2])
                    eng.tensor_scalar_add(scr[:, 36:48], eb16[:, 4:16], 1.0)
                    phi()
                    eng.reciprocal(scr[:, 0:12], scr[:, 36:48])
                    phi()
                    eng.tensor_mul(scr[:, 16:20], scr[:, 4:8], Ps[:, :])
                    eng.tensor_scalar(scr[:, 12:16], scr[:, 0:4], 2.0, -1.0,
                                      ALU.mult, ALU.add)
                    eng.tensor_max(Ps[:, :], eb16[:, 0:4], scr[:, 16:20])
                    phi()
                    eng.reciprocal(scr[:, 20:24], Ps[:, :])
                    phi()
                    eng.tensor_mul(scr[:, 24:28], eb16[:, 0:4], scr[:, 20:24])
                    eng.tensor_mul(scr[:, 28:32], scr[:, 16:20], scr[:, 20:24])
                    eng.tensor_mul(scr[:, 16:20], scr[:, 24:28], scr[:, 12:16])
                    eng.tensor_mul(cs[:, :], scr[:, 28:32], cs[:, :])
                    eng.tensor_mul(ns[:, :], scr[:, 28:32], ns[:, :])
                    eng.tensor_add(cs[:, :], cs[:, :], scr[:, 16:20])
                    eng.tensor_add(ns[:, :], ns[:, :], scr[:, 24:28])
                    eng.tensor_mul(scr[:, 16:20], scr[:, 8:12], cs[:, :])
                    eng.reciprocal(scr[:, 20:24], ns[:, :])
                    phi()
                    emit(eng.tensor_mul(ysb[:, t * 4:(t + 1) * 4],
                                        scr[:, 16:20], scr[:, 20:24]))
                sc.v(v_cell)
                if dbg == "tap":
                    def y_tap(eng, emit, t=t, ci=ci):
                        tg = (ci * TCH + t) * 96
                        emit(eng.dma_start(out=tapd[:, tg:tg + 16], in_=rawb[:, :]), 16)
                        emit(eng.dma_start(out=tapd[:, tg + 16:tg + 32], in_=eb16[:, :]), 16)
                        emit(eng.dma_start(out=tapd[:, tg + 32:tg + 64], in_=scr[:, :]), 16)
                        emit(eng.dma_start(out=tapd[:, tg + 64:tg + 68], in_=hc[:, :]), 16)
                        emit(eng.dma_start(out=tapd[:, tg + 68:tg + 72], in_=cs[:, :]), 16)
                        emit(eng.dma_start(out=tapd[:, tg + 72:tg + 76], in_=ns[:, :]), 16)
                        emit(eng.dma_start(out=tapd[:, tg + 76:tg + 80], in_=Ps[:, :]), 16)
                        emit(eng.dma_start(out=tapd[:, tg + 80:tg + 96],
                                           in_=wxb[:, t * 16:(t + 1) * 16]), 16)
                    sc.y(y_tap, 16 * 8)

            def v_hc(eng, emit):
                emit(eng.tensor_copy(hc[:, :], ysb[:, C4 - 4:C4]))
            sc.v(v_hc)

            if dbg in ("nogn", "ebnogn"):
                def g_yst0(eng, emit, ci=ci):
                    emit(eng.dma_start(out=ygnd[:, ci * C4:(ci + 1) * C4],
                                       in_=ysb[:, :]), 16)
                sc.y(g_yst0, 16)
                continue
            def s_gsq(eng, emit):
                emit(eng.activation(sqg[:, :], ysb[:, :], AF.Square))
            sc.s(s_gsq)
            for c0, w in _chunks(C4):
                def t_gst(eng, emit, c0=c0, w=w):
                    eng.matmul(psS[:, 0:w], ones[:, :], ysb[:, c0:c0 + w],
                               start=True, stop=True)
                    emit(eng.matmul(psQ[:, 0:w], ones[:, :], sqg[:, c0:c0 + w],
                                    start=True, stop=True))
                sc.t(t_gst)

                def v_gcp(eng, emit, c0=c0, w=w):
                    eng.tensor_copy(gst[:, c0:c0 + w], psS[:, 0:w])
                    emit(eng.tensor_copy(gst[:, C4 + c0:C4 + c0 + w], psQ[:, 0:w]))
                sc.v(v_gcp)

            def v_gnstat(eng, emit):
                sA = gst[:, 0:C4].rearrange("r (t h b) -> r t h b", h=2, b=2)
                suv = gtm[:, 0:TCH * 2].rearrange("r (t b) -> r t b", b=2)
                eng.tensor_add(suv, sA[:, :, 0, :], sA[:, :, 1, :])
                qA = gst[:, C4:2 * C4].rearrange("r (t h b) -> r t h b", h=2, b=2)
                quv = gtm[:, C4:C4 + TCH * 2].rearrange("r (t b) -> r t b", b=2)
                eng.tensor_add(quv, qA[:, :, 0, :], qA[:, :, 1, :])
                n2 = TCH * 2
                mu = gab[:, 0:n2]
                vv = gab[:, C4:C4 + n2]
                eng.tensor_scalar_mul(mu, gtm[:, 0:n2], 1.0 / DH)
                eng.tensor_scalar_mul(vv, gtm[:, C4:C4 + n2], 1.0 / DH)
                eng.tensor_mul(gtm[:, 0:n2], mu, mu)
                eng.tensor_sub(vv, vv, gtm[:, 0:n2])
                emit(eng.tensor_scalar_add(vv, vv, EPS))
            sc.v(v_gnstat)

            def s_gsqrt(eng, emit):
                n2 = TCH * 2
                emit(eng.activation(gtm[:, 0:n2], gab[:, C4:C4 + n2], AF.Ln))
            sc.s(s_gsqrt)

            def s_gexp(eng, emit):
                n2 = TCH * 2
                emit(eng.activation(gtm[:, 0:n2], gtm[:, 0:n2], AF.Exp, scale=-0.5))
            sc.s(s_gexp)

            def v_gnewt(eng, emit):
                n2 = TCH * 2
                vv = gab[:, C4:C4 + n2]
                rs = gtm[:, 0:n2]
                t0 = gtm[:, C4:C4 + n2]
                eng.tensor_mul(t0, rs, rs)
                eng.tensor_mul(t0, vv, t0)
                eng.tensor_scalar(t0, t0, -0.5, 1.5, ALU.mult, ALU.add)
                eng.tensor_mul(rs, rs, t0)
                eng.tensor_mul(gtm[:, C4:C4 + n2], gab[:, 0:n2], rs)
                eng.tensor_scalar_mul(gtm[:, C4:C4 + n2], gtm[:, C4:C4 + n2], -1.0)
                a2 = gtm[:, 0:n2].rearrange("r (t b) -> r t b", b=2)
                a2 = a2.unsqueeze(2).broadcast_to([1, TCH, 2, 2])
                b2 = gtm[:, C4:C4 + n2].rearrange("r (t b) -> r t b", b=2)
                b2 = b2.unsqueeze(2).broadcast_to([1, TCH, 2, 2])
                arow = grow[:, 0:C4].rearrange("r (t h b) -> r t h b", h=2, b=2)
                brow = grow[:, C4:2 * C4].rearrange("r (t h b) -> r t h b", h=2, b=2)
                eng.tensor_copy(arow, a2)
                emit(eng.tensor_copy(brow, b2))
            sc.v(v_gnewt)

            for c0, w in _chunks(C4):
                def t_gbc(eng, emit, c0=c0, w=w):
                    eng.matmul(psA[:, 0:w], onerow[:, :], grow[:, c0:c0 + w],
                               start=True, stop=True)
                    emit(eng.matmul(psB[:, 0:w], onerow[:, :],
                                    grow[:, C4 + c0:C4 + c0 + w],
                                    start=True, stop=True))
                sc.t(t_gbc)

                def v_gapp(eng, emit, c0=c0, w=w):
                    eng.tensor_mul(ysb[:, c0:c0 + w], ysb[:, c0:c0 + w], psA[:, 0:w])
                    eng.tensor_add(ysb[:, c0:c0 + w], ysb[:, c0:c0 + w], psB[:, 0:w])
                    for hh in range(2):
                        yv = ysb[:, c0:c0 + w].rearrange("p (t h b) -> p t h b",
                                                         h=2, b=2)[:, :, hh, :]
                        i = eng.tensor_single_scalar(yv, yv, gnw[:, hh:hh + 1],
                                                     ALU.mult)
                    emit(i)
                sc.v(v_gapp)

            def g_yst(eng, emit, ci=ci):
                emit(eng.dma_start(out=ygnd[:, ci * C4:(ci + 1) * C4],
                                   in_=ysb[:, :]), 16)
            sc.y(g_yst, 16)

        def v_fin(eng, emit):
            eng.tensor_copy(stb[:, 0:4], hc[:, :])
            eng.tensor_copy(stb[:, 4:8], cs[:, :])
            eng.tensor_copy(stb[:, 8:12], ns[:, :])
            emit(eng.tensor_copy(stb[:, 12:16], Ps[:, :]))
        sc.v(v_fin)

        def g_fin(eng, emit):
            emit(eng.dma_start(out=sto[:, :], in_=stb[:, :]), 16)
        sc.y(g_fin, 16)
        emit_script(nc, es, sc)
    return nc



def build_ffn():
    CCW = 256
    GV = [(j * 128, 128) for j in range(10)] + [(1280, 64)]
    KD = [(k * 128, 128) for k in range(10)] + [(1280, 64)]
    nc = bass.Bass("TRN2", target_bir_lowering=False,
                   detect_race_conditions=False)
    xa = nc.dram_tensor("xa", [D, NT], F32, kind="ExternalInput")
    yg = nc.dram_tensor("yg", [D, NT], F32, kind="ExternalInput")
    ln2w = nc.dram_tensor("ln2w", [D], F32, kind="ExternalInput")
    ffup = nc.dram_tensor("ffup", [D, 2 * UP], F32, kind="ExternalInput")
    ffdn = nc.dram_tensor("ffdn", [UP, D], F32, kind="ExternalInput")
    xo = nc.dram_tensor("xo", [D, NT], F32, kind="ExternalOutput")
    with contextlib.ExitStack() as es:
        al = Alloc(nc, es)
        sc = Script()
        fut = al.sb("fut", [128, PC * 2 * UP])
        fdt = al.sb("fdt", [128, 11 * D])
        lnw_s = al.sb("lnw_s", [128, PC])
        xac = al.sb("xac", [128, PC * CCW])
        xnc = al.sb("xnc", [128, PC * CCW])
        hidt = al.sb("hidt", [128, 11 * CCW])
        gj = al.sb("gj", [128, CCW])
        psU = al.ps("psU", [128, 512])
        psV = al.ps("psV", [128, 512])
        psD = al.ps("psD", [128, 512])
        ones, onerow = add_consts(sc, al)
        sh = make_ln_shared(al, CCW)

        def y_w(eng, emit):
            emit(eng.dma_start(out=lnw_s[:, :],
                               in_=ln2w[:].rearrange("(c p) -> p c", p=128),
                               allow_slow_non_contiguous=True), 16)
            for kc in range(PC):
                emit(eng.dma_start(out=fut[:, kc * 2 * UP:(kc + 1) * 2 * UP],
                                   in_=ffup[kc * 128:(kc + 1) * 128, :]), 16)
            for kc, (r0, kk) in enumerate(KD):
                emit(eng.dma_start(out=fdt[0:kk, kc * D:(kc + 1) * D],
                                   in_=ffdn[r0:r0 + kk, :]), 16)
        sc.y(y_w, 16 * (1 + PC + 11))

        for c0, w in _chunks(NT, CCW):
            def y_x(eng, emit, c0=c0, w=w):
                for pc in range(PC):
                    emit(eng.dma_start(out=xac[:, pc * CCW:pc * CCW + w],
                                       in_=xa[pc * 128:(pc + 1) * 128, c0:c0 + w]), 16)
                for pc in range(PC):
                    emit(eng.dma_start(out=xnc[:, pc * CCW:pc * CCW + w],
                                       in_=yg[pc * 128:(pc + 1) * 128, c0:c0 + w]), 16)
            sc.y(y_x, 16 * 2 * PC)

            def v_res(eng, emit, w=w):
                for pc in range(PC):
                    i = eng.tensor_add(xac[:, pc * CCW:pc * CCW + w],
                                       xac[:, pc * CCW:pc * CCW + w],
                                       xnc[:, pc * CCW:pc * CCW + w])
                emit(i)
            sc.v(v_res)
            add_ln(sc, sh, xac, xnc, w, PC, ones, onerow, w_sb=lnw_s)
            for j, (jb, M) in enumerate(GV):
                def t_up(eng, emit, jb=jb, M=M, w=w):
                    for kc in range(PC):
                        eng.matmul(psU[0:M, 0:w],
                                   fut[:, kc * 2 * UP + jb:kc * 2 * UP + jb + M],
                                   xnc[:, kc * CCW:kc * CCW + w],
                                   start=(kc == 0), stop=(kc == PC - 1))
                    for kc in range(PC):
                        i = eng.matmul(psV[0:M, 0:w],
                                       fut[:, kc * 2 * UP + UP + jb:kc * 2 * UP + UP + jb + M],
                                       xnc[:, kc * CCW:kc * CCW + w],
                                       start=(kc == 0), stop=(kc == PC - 1))
                    emit(i)
                sc.t(t_up)

                def s_gelu(eng, emit, j=j, M=M, w=w):
                    emit(eng.activation(gj[0:M, 0:w], psU[0:M, 0:w], AF.Gelu))
                sc.s(s_gelu)

                def v_hmul(eng, emit, j=j, M=M, w=w):
                    emit(eng.tensor_mul(hidt[0:M, j * CCW:j * CCW + w],
                                        gj[0:M, 0:w], psV[0:M, 0:w]))
                sc.v(v_hmul)
            for nj in range(PC):
                def t_dn(eng, emit, nj=nj, w=w):
                    for kc, (r0, kk) in enumerate(KD):
                        i = eng.matmul(psD[:, 0:w],
                                       fdt[0:kk, kc * D + nj * 128:kc * D + (nj + 1) * 128],
                                       hidt[0:kk, kc * CCW:kc * CCW + w],
                                       start=(kc == 0), stop=(kc == 10))
                    emit(i)
                sc.t(t_dn)

                def v_out(eng, emit, nj=nj, w=w):
                    emit(eng.tensor_add(xac[:, nj * CCW:nj * CCW + w],
                                        xac[:, nj * CCW:nj * CCW + w], psD[:, 0:w]))
                sc.v(v_out)

            def g_st(eng, emit, c0=c0, w=w):
                for pc in range(PC):
                    emit(eng.dma_start(out=xo[pc * 128:(pc + 1) * 128, c0:c0 + w],
                                       in_=xac[:, pc * CCW:pc * CCW + w]), 16)
            sc.y(g_st, 16 * PC)
        emit_script(nc, es, sc)
    return nc



def build_final():
    CCW = 512
    nc = bass.Bass("TRN2", target_bir_lowering=False,
                   detect_race_conditions=False)
    xa4 = nc.dram_tensor("xa4", [D, NT], F32, kind="ExternalInput")
    postw = nc.dram_tensor("postw", [D], F32, kind="ExternalInput")
    g2d = nc.dram_tensor("g2", [D, NT], F32, kind="ExternalInput")
    xind = nc.dram_tensor("xin", [D, NT], F32, kind="ExternalInput")
    outd = nc.dram_tensor("out", [D, NT], F32, kind="ExternalOutput")
    with contextlib.ExitStack() as es:
        al = Alloc(nc, es)
        sc = Script()
        xt = al.sb("xt", [128, PC * CCW])
        st = al.sb("st", [128, PC * CCW])
        g2t = al.sb("g2t", [128, PC * CCW])
        xit = al.sb("xit", [128, PC * CCW])
        pw_s = al.sb("pw_s", [128, PC])
        ones, onerow = add_consts(sc, al)
        sh = make_ln_shared(al, CCW)

        def y_w(eng, emit):
            emit(eng.dma_start(out=pw_s[:, :],
                               in_=postw[:].rearrange("(c p) -> p c", p=128),
                               allow_slow_non_contiguous=True), 16)
        sc.y(y_w, 16)
        for c0, w in _chunks(NT, CCW):
            def y_x(eng, emit, c0=c0, w=w):
                for pc in range(PC):
                    emit(eng.dma_start(out=xt[:, pc * CCW:pc * CCW + w],
                                       in_=xa4[pc * 128:(pc + 1) * 128, c0:c0 + w]), 16)
                    emit(eng.dma_start(out=g2t[:, pc * CCW:pc * CCW + w],
                                       in_=g2d[pc * 128:(pc + 1) * 128, c0:c0 + w]), 16)
                    emit(eng.dma_start(out=xit[:, pc * CCW:pc * CCW + w],
                                       in_=xind[pc * 128:(pc + 1) * 128, c0:c0 + w]), 16)
            sc.y(y_x, 16 * 3 * PC)
            add_ln(sc, sh, xt, st, w, PC, ones, onerow, w_sb=pw_s)

            def v_comb(eng, emit, w=w):
                for pc in range(PC):
                    sl = slice(pc * CCW, pc * CCW + w)
                    eng.scalar_tensor_tensor(st[:, sl], st[:, sl], 0.5, g2t[:, sl],
                                             ALU.mult, ALU.mult)
                    i = eng.tensor_add(st[:, sl], st[:, sl], xit[:, sl])
                emit(i)
            sc.v(v_comb)

            def g_st(eng, emit, c0=c0, w=w):
                for pc in range(PC):
                    emit(eng.dma_start(out=outd[pc * 128:(pc + 1) * 128, c0:c0 + w],
                                       in_=st[:, pc * CCW:pc * CCW + w]), 16)
            sc.y(g_st, 16 * PC)
        emit_script(nc, es, sc)
    return nc



_PROGS = {}


def _get(key, builder, *a):
    if key not in _PROGS:
        _PROGS[key] = builder(*a)
    return _PROGS[key]


def _run(nc, in_maps):
    r = run_bass_kernel_spmd(nc, in_maps, core_ids=list(range(NCORES)))
    return r.results


def _tok_slices(arr):
    return [np.ascontiguousarray(arr[c // 2, (c % 2) * NT:(c % 2 + 1) * NT, :].T)
            for c in range(NCORES)]


def _tok_unslices(slices, C=D):
    out = np.empty((B, S, C), np.float32)
    for c in range(NCORES):
        out[c // 2, (c % 2) * NT:(c % 2 + 1) * NT, :] = slices[c].T
    return out


def kernel(L0_attn_x, L0_attn_ln1_w, L0_attn_conv_w, L0_attn_conv_b,
           L0_attn_wi, L0_attn_wf, L0_attn_wz, L0_attn_wo, L0_attn_rker,
           L0_attn_cell_b, L0_attn_gn_w, L0_attn_ln2_w, L0_attn_ff_up,
           L0_attn_ff_down, L0_attn_post_w, L0_ffn_w_gate, L0_ffn_w_up,
           L0_ffn_w_down):
    f32 = lambda a: np.ascontiguousarray(np.asarray(a), dtype=np.float32)
    x = f32(L0_attn_x)
    ln1_w, conv_w, conv_b = f32(L0_attn_ln1_w), f32(L0_attn_conv_w), f32(L0_attn_conv_b)
    wi, wf, wz, wo = f32(L0_attn_wi), f32(L0_attn_wf), f32(L0_attn_wz), f32(L0_attn_wo)
    rker, cell_b = f32(L0_attn_rker), f32(L0_attn_cell_b)
    gn_w, ln2_w = f32(L0_attn_gn_w), f32(L0_attn_ln2_w)
    ff_up, ff_down, post_w = f32(L0_attn_ff_up), f32(L0_attn_ff_down), f32(L0_attn_post_w)
    w_gate, w_up, w_down = f32(L0_ffn_w_gate), f32(L0_ffn_w_up), f32(L0_ffn_w_down)

    xs = _tok_slices(x)
    res = _run(_get("prep", build_prep), [{"x": s} for s in xs])
    xn = _tok_unslices([r["xn"] for r in res])

    wgp = np.zeros((3072, D), np.float32)
    wgp[:FFI] = w_gate
    wup = np.zeros((3072, D), np.float32)
    wup[:FFI] = w_up
    nc_q3 = _get("tmq3_1024", build_tmq, 3, 1024)
    res = _run(nc_q3, [{"a": np.ascontiguousarray(wgp[c * 384:(c + 1) * 384])}
                       for c in range(NCORES)])
    wgq = np.concatenate([r["aq"] for r in res], 0)[:FFI]
    res = _run(nc_q3, [{"a": np.ascontiguousarray(wup[c * 384:(c + 1) * 384])}
                       for c in range(NCORES)])
    wuq = np.concatenate([r["aq"] for r in res], 0)[:FFI]
    nc_q1 = _get("tmq1_2816", build_tmq, 1, FFI)
    res = _run(nc_q1, [{"a": np.ascontiguousarray(w_down[c * 128:(c + 1) * 128])}
                       for c in range(NCORES)])
    wdq = np.concatenate([r["aq"] for r in res], 0)

    nc_qx = _get("tmq16_1024", build_tmq, 16, 1024)
    res = _run(nc_qx, [{"a": np.ascontiguousarray(xn[c // 2, (c % 2) * NT:(c % 2 + 1) * NT, :])}
                       for c in range(NCORES)])
    xq = _tok_unslices([np.ascontiguousarray(r["aq"].T) for r in res])
    wgqT = np.ascontiguousarray(wgq.T)
    wuqT = np.ascontiguousarray(wuq.T)
    res = _run(_get("fp4gu", build_fp4gu),
               [{"xq": s, "wgq": wgqT, "wuq": wuqT} for s in _tok_slices(xq)])
    h_full = _tok_unslices([r["h4"] for r in res], C=FFI)
    nc_qh = _get("tmq16_2816", build_tmq, 16, FFI)
    res = _run(nc_qh, [{"a": np.ascontiguousarray(h_full[c // 2, (c % 2) * NT:(c % 2 + 1) * NT, :])}
                       for c in range(NCORES)])
    hq = _tok_unslices([np.ascontiguousarray(r["aq"].T) for r in res], C=FFI)
    wdqT = np.ascontiguousarray(wdq.T)
    res = _run(_get("fp4dn", build_fp4dn),
               [{"hq": s, "wdq": wdqT} for s in _tok_slices(hq)])
    g2_sl = [r["g2"] for r in res]

    xa = xn
    T2 = S // 2
    for l in range(4):
        wstk = np.stack([wi[l].transpose(0, 2, 1), wz[l].transpose(0, 2, 1),
                         wf[l].transpose(0, 2, 1), wo[l].transpose(0, 2, 1)], 1)
        cellb_p = np.ascontiguousarray(cell_b[l][:, [0, 2, 1, 3], :])
        in_maps = []
        for c in range(NCORES):
            b, th = c // 2, c % 2
            t0 = th * NT
            halo = (np.zeros((3, D), np.float32) if th == 0
                    else xa[b, t0 - 3:t0, :])
            xasl = np.ascontiguousarray(
                np.concatenate([halo, xa[b, t0:t0 + NT, :]], 0).T)
            in_maps.append({"xa": xasl, "ln1w": ln1_w[l], "convw": conv_w[l],
                            "convb": conv_b[l], "wstk": np.ascontiguousarray(wstk),
                            "cellb": cellb_p})
        res = _run(_get("lncg", build_lncg), in_maps)
        wx_full = np.empty((H, 8, 128, B, S), np.float32)
        for c in range(NCORES):
            b, th = c // 2, c % 2
            wx_full[:, :, :, b, th * NT:(th + 1) * NT] = \
                res[c]["wx"].reshape(H, 8, 128, NT)
        nc_scan = _get("scan", build_scan, T2, 512)
        wx_sc, r_sc, gnw_sc = [], [], []
        for c in range(NCORES):
            h, bp = c // 2, c % 2
            arr = wx_full[h, :, :, 2 * bp:2 * bp + 2, :]
            wx_sc.append(np.ascontiguousarray(
                arr.transpose(1, 3, 0, 2).reshape(128, S * 16)))
            r_sc.append(np.ascontiguousarray(
                rker[l, h][:, [0, 2, 1, 3], :].reshape(DH, 1024)))
            gnw_sc.append(np.ascontiguousarray(
                gn_w[l, h * DH:(h + 1) * DH].reshape(2, 128).T))
        st0 = np.zeros((128, 16), np.float32)
        st0[:, 12:16] = 1.0
        res1 = _run(nc_scan, [{"wx": wx_sc[c][:, :T2 * 16], "r": r_sc[c],
                               "gnw": gnw_sc[c], "statei": st0}
                              for c in range(NCORES)])
        res2 = _run(nc_scan, [{"wx": np.ascontiguousarray(wx_sc[c][:, T2 * 16:]),
                               "r": r_sc[c], "gnw": gnw_sc[c],
                               "statei": res1[c]["stateo"]}
                              for c in range(NCORES)])
        y_full = np.empty((B, S, D), np.float32)
        for c in range(NCORES):
            h, bp = c // 2, c % 2
            for part, resx in ((0, res1), (1, res2)):
                yg = resx[c]["ygn"].reshape(128, T2, 2, 2)
                y_full[2 * bp:2 * bp + 2, part * T2:(part + 1) * T2,
                       h * DH:(h + 1) * DH] = \
                    yg.transpose(3, 1, 2, 0).reshape(2, T2, DH)
        res = _run(_get("ffn", build_ffn),
                   [{"xa": sa, "yg": sy, "ln2w": ln2_w[l], "ffup": ff_up[l],
                     "ffdn": ff_down[l]}
                    for sa, sy in zip(_tok_slices(xa), _tok_slices(y_full))])
        xa = _tok_unslices([r["xo"] for r in res])

    res = _run(_get("final", build_final),
               [{"xa4": sa, "postw": post_w, "g2": g2_sl[c], "xin": sx}
                for c, (sa, sx) in enumerate(zip(_tok_slices(xa), _tok_slices(x)))])
    out = _tok_unslices([r["out"] for r in res])
    return out
